# version 22 (final)
# baseline (speedup 1.0000x reference)
import math
import numpy as np
import ml_dtypes
import concourse.bass as bass
import concourse.mybir as mybir
from concourse.bass import IndirectOffsetOnAxis
from concourse.bass_utils import run_bass_kernel_spmd
from concourse.alu_op_type import AluOpType
from concourse.replica_groups import maybe_share_collective_output_space

F32 = mybir.dt.float32
BF16 = mybir.dt.bfloat16
U32 = mybir.dt.uint32
AF = mybir.ActivationFunctionType
AX = mybir.AxisListType
ADD = AluOpType.add
MUL = AluOpType.mult

NCORES = 8
D = 2048
H = 16
HD = 128
T = 2048
SQ = 8192
DH = 8192
E3 = 3 * D
ISC = 1.0 / math.sqrt(HD)
EPS = 1e-5
NEGBIG = -30000.0
KVQ_ROWS = 3 * D + 32
PAIRS_PER_SLOT = sum(2 * (qs + 1) for qs in range(16))
TOTAL_PAIRS = 4 * PAIRS_PER_SLOT


class Sem:
    def __init__(self, h):
        self.h = h
        self.n = 0

    def bump(self, k=1):
        self.n += k
        return self.n


def build(dbg=(), phase_stop=99):
    nc = bass.Bass("TRN2", target_bir_lowering=False, debug=False)

    def kind_of(name):
        return "ExternalOutput" if name in dbg else "Internal"

    x_in = nc.dram_tensor("x", [T, D], F32, kind="ExternalInput").ap()
    wqkvT = nc.dram_tensor("wqkvT", [D, E3], F32, kind="ExternalInput").ap()
    woutT = nc.dram_tensor("woutT", [D, D], F32, kind="ExternalInput").ap()
    wgT = nc.dram_tensor("wgT", [D, DH], F32, kind="ExternalInput").ap()
    wuT = nc.dram_tensor("wuT", [D, DH], F32, kind="ExternalInput").ap()
    wdT = nc.dram_tensor("wdT", [DH, D], F32, kind="ExternalInput").ap()
    bq_in = nc.dram_tensor("bqd", [128, 48], F32, kind="ExternalInput").ap()
    bor_in = nc.dram_tensor("bord", [1, D], F32, kind="ExternalInput").ap()
    masks_in = nc.dram_tensor("masksd", [128, 2048], BF16, kind="ExternalInput").ap()
    idnb_in = nc.dram_tensor("idnbd", [128, 128], BF16, kind="ExternalInput").ap()
    idn_in = nc.dram_tensor("idnd", [128, 128], F32, kind="ExternalInput").ap()
    onec_in = nc.dram_tensor("onecd", [128, 1], BF16, kind="ExternalInput").ap()
    oner_in = nc.dram_tensor("onerd", [1, 128], F32, kind="ExternalInput").ap()
    idx_in = nc.dram_tensor("idxd", [128, 72], U32, kind="ExternalInput").ap()
    out_d = nc.dram_tensor("out", [T, D], F32, kind="ExternalOutput").ap()

    space = maybe_share_collective_output_space("AllGather", [list(range(NCORES))])
    kvq_t = nc.dram_tensor("kvq_in", [KVQ_ROWS, D], BF16, kind=kind_of("kvq_in"))
    g1e_t = nc.dram_tensor("g1e", [NCORES * 3088, D], BF16, kind="Internal",
                           addr_space=space)
    g1o_t = nc.dram_tensor("g1o", [NCORES * 3088, D], BF16, kind="Internal",
                           addr_space=space)
    in2_t = nc.dram_tensor("in2", [512, 2 * SQ], BF16, kind=kind_of("in2"))
    g2e_t = nc.dram_tensor("g2e", [NCORES * 256, 2 * SQ], BF16, kind="Internal",
                           addr_space=space)
    g2o_t = nc.dram_tensor("g2o", [NCORES * 256, 2 * SQ], BF16, kind="Internal",
                           addr_space=space)
    x2b_t = nc.dram_tensor("x2b", [T, D], F32, kind=kind_of("x2b"))
    woutb = nc.dram_tensor("woutb", [D, D], BF16, kind="Internal").ap()
    wgb = nc.dram_tensor("wgb", [D, DH], BF16, kind="Internal").ap()
    wub = nc.dram_tensor("wub", [D, DH], BF16, kind="Internal").ap()
    wdb = nc.dram_tensor("wdb", [DH, D], BF16, kind="Internal").ap()
    nrm_t = nc.dram_tensor("nrm_d", [1, 64], F32, kind="Internal")
    kvq = kvq_t.ap()
    g1e = g1e_t.ap()
    g1o = g1o_t.ap()
    in2 = in2_t.ap()
    g2e = g2e_t.ap()
    g2o = g2o_t.ap()
    x2b = x2b_t.ap()
    nrm_d = nrm_t.ap()

    SB0 = 16512
    def sbt(name, shape, dtype, off):
        off = off + SB0
        assert off + shape[-1] * {F32: 4, BF16: 2, U32: 4}[dtype] * (
            1 if len(shape) == 2 else shape[1]) <= 229344, name
        return nc.alloc_sbuf_tensor_at(name, shape, dtype, offset=off).ap()

    masks = sbt("masks", [128, 2048], BF16, 0)
    idnb = sbt("idnb", [128, 128], BF16, 4096)
    bob = sbt("bob", [128, 2048], F32, 8192)
    idn = sbt("idn", [128, 128], F32, 16384)
    onec = sbt("onec", [128, 1], BF16, 16896)
    oner = sbt("oner", [1, 128], F32, 16928)
    bq = sbt("bq", [128, 48], F32, 17440)
    bqs = sbt("bqs", [128, 16], F32, 17664)
    bor = sbt("bor", [1, D], F32, 17728)
    idxs = sbt("idxs", [128, 72], U32, 25920)
    stats = sbt("stats", [128, 80], F32, 26240)
    nrow = sbt("nrow", [1, 2048], F32, 26560)
    sml = sbt("sml", [8, 16], F32, 34752)
    BASE = 34848

    xln1 = sbt("xln1", [128, 16 * T], BF16, BASE)
    wsb = [sbt(f"wsb{i}", [128, 2048], BF16, BASE + 65536 + 4096 * i) for i in range(2)]
    stg = [sbt(f"stg{i}", [128, 2048], BF16, BASE + 73728 + 4096 * i) for i in range(2)]
    vstage = sbt("vstage", [128, 2048], F32, BASE + 81920)
    q2scr = [sbt(f"q2s{i}", [128, 2048], BF16, BASE + 90112 + 4096 * i) for i in range(2)]
    xt = [sbt(f"xt{i}", [128, 2048], F32, BASE + 98304 + 8192 * i) for i in range(2)]
    lnscr = sbt("lnscr", [128, 2048], F32, BASE + 114688)

    k_sb = [sbt(f"ksb{i}", [128, SQ], BF16, BASE + 16384 * i) for i in range(2)]
    v_sb = [sbt(f"vsb{i}", [128, SQ], BF16, BASE + 32768 + 16384 * i) for i in range(2)]
    q_sb = [sbt(f"qsb{i}", [128, SQ], BF16, BASE + 65536 + 16384 * i) for i in range(2)]
    ast = [sbt(f"ast{i}", [128, SQ], BF16, BASE + 98304 + 16384 * i) for i in range(2)]
    pt = [sbt(f"pt{i}", [128, 1024], BF16, BASE + 131072 + 2048 * i) for i in range(4)]
    linb = sbt("linb", [128, 512], F32, BASE + 139264)
    kq2 = sbt("kq2", [8, 2 * 2048], BF16, BASE + 141312)
    negm = sbt("negm", [128, 1], F32, BASE + 149504)
    linv = sbt("linv", [1, 512], F32, 26560 - 16512)

    afm = sbt("afm", [128, 16 * T], BF16, BASE)
    wout = sbt("wout", [128, 16 * D], BF16, BASE + 65536)
    xres = [sbt(f"xres{i}", [128, 2048], F32, BASE + 131072 + 8192 * i) for i in range(2)]
    x2st = sbt("x2st", [128, 2048], F32, BASE + 147456)

    y2 = sbt("y2", [128, 8 * D], F32, BASE)
    xln2 = sbt("xln2", [128, 16 * 1024], BF16, BASE + 65536)
    gated = sbt("gated", [128, 16 * 1024], BF16, BASE + 98304)
    gu = [sbt(f"gu{i}", [128, 8192], BF16, BASE + 131072 + 16384 * i) for i in range(2)]
    pcst = [sbt(f"pcst{i}", [128, 4096], BF16, BASE + 155648 + 8192 * i) for i in range(2)]
    lnscr2 = sbt("lnscr2", [128, 2048], F32, BASE + 163840)
    dw = [sbt(f"dw{i}", [128, 512], BF16, BASE + 172032 + 1024 * i) for i in range(2)]

    psA = nc.alloc_psum_tensor("psA", [128, 2048], F32).ap()
    psB = nc.alloc_psum_tensor("psB", [128, 2048], F32).ap()

    ops = {k: [] for k in ("tensor", "vector", "scalar", "gpsimd", "sync")}
    sems = {}

    def S(name):
        if name not in sems:
            sems[name] = Sem(nc.semaphore(name).__enter__())
        return sems[name]

    def em(eng, fn):
        ops[eng].append(fn)

    def wt(eng, sem, v):
        if v > 0:
            em(eng, lambda E, h=sem.h, vv=v: E.wait_ge(h, vv))

    def run(eng, fn, sem=None, k=1):
        if sem is None:
            em(eng, fn)
        else:
            sem.bump(k)
            em(eng, lambda E, f=fn, h=sem.h, kk=k: f(E).then_inc(h, kk))

    def MM(out, lhsT, rhs, start, stop, sem=None, k=1):
        run("tensor", lambda E, o=out, l=lhsT, r=rhs, s1=start, s2=stop:
            E.matmul(o, l, r, start=s1, stop=s2), sem, k)

    def TR(out, in_, sem=None, k=1):
        run("tensor", lambda E, o=out, i=in_: E.transpose(o, i, idn), sem, k)

    def ACT(out, in_, func, bias=0.0, scale=1.0, accum=None, sem=None, k=1):
        run("scalar", lambda E, o=out, i=in_, f=func, b=bias, s=scale, a=accum:
            E.activation(o, i, f, bias=b, scale=s, accum_out=a), sem, k)

    def TT(eng, out, in0, in1, op, sem=None, k=1):
        run(eng, lambda E, o=out, a=in0, b=in1, p=op: E.tensor_tensor(o, a, b, p), sem, k)

    def TS(out, in0, s1, s2, op0, op1=None, accum=None, sem=None, k=1):
        if op1 is None:
            run("vector", lambda E, o=out, i=in0, a=s1, p0=op0:
                E.tensor_scalar(o, i, a, None, p0), sem, k)
        else:
            run("vector", lambda E, o=out, i=in0, a=s1, b=s2, p0=op0, p1=op1, ac=accum:
                E.tensor_scalar(o, i, a, b, p0, p1, accum_out=ac), sem, k)

    def RED(out, in_, op, sem=None, k=1):
        run("vector", lambda E, o=out, i=in_, p=op: E.tensor_reduce(o, i, AX.X, p), sem, k)

    def CPY(eng, out, in_, sem=None, k=1):
        run(eng, lambda E, o=out, i=in_: E.tensor_copy(o, i), sem, k)

    def RCP(out, in_, sem=None, k=1):
        run("vector", lambda E, o=out, i=in_: E.reciprocal(o, i), sem, k)

    def DMA(eng, out, in_, sem=None):
        run(eng, lambda E, o=out, i=in_: E.dma_start(out=o, in_=i), sem, 16)

    def IDMA(out, src, idx_ap, axis, sem=None):
        run("gpsimd", lambda E, o=out, s=src, ia=idx_ap, ax=axis:
            E.indirect_dma_start(out=o, out_offset=None, in_=s,
                                 in_offset=IndirectOffsetOnAxis(ia, ax)), sem, 16)

    s_cst = S("cst")
    s_x = S("x")
    s_red = S("red")
    s_sqa = S("sqa")
    s_stt = S("stt")
    s_srt = S("srt")
    s_nrm = S("nrm")
    s_tr = S("tr")
    s_tre = S("tre")
    s_wld = S("wld")
    s_qps = S("qps")
    s_ev = S("ev")
    s_st2 = S("st2")
    s_sqr = S("sqr")
    s_nq = S("nq")
    s_nev = S("nev")
    s_nst = S("nst")
    s_vt = S("vt")
    s_vte = S("vte")
    s_cc1 = S("cc1")
    s_kvg = S("kvg")
    s_mch = S("mch")
    s_sq = S("sq")
    s_msk = S("msk")
    s_ex = S("ex")
    s_av = S("av")
    s_rcp = S("rcp")
    s_bc = S("bc")
    s_lev = S("lev")
    s_div = S("div")
    s_ast = S("ast")
    s_cc2 = S("cc2")
    s_pg = S("pg")
    s_bcb = S("bcb")
    s_y = S("y")
    s_a1 = S("a1")
    s_a2 = S("a2")
    s_xr = S("xr")
    s_x2s = S("x2s")
    s_x2l = S("x2l")
    s_gld = S("gld")
    s_gmm = S("gmm")
    s_sil = S("sil")
    s_gtd = S("gtd")
    s_dld = S("dld")
    s_dmm = S("dmm")
    s_ye = S("ye")
    s_out = S("out")

    DMA("sync", masks[:, :], masks_in[:, :], s_cst)
    DMA("sync", idnb[:, :], idnb_in[:, :], s_cst)
    DMA("sync", idn[:, :], idn_in[:, :], s_cst)
    DMA("sync", onec[:, :], onec_in[:, :], s_cst)
    DMA("sync", oner[:, :], oner_in[:, :], s_cst)
    DMA("sync", bq[:, :], bq_in[:, :], s_cst)
    DMA("sync", bor[:, :], bor_in[:, :], s_cst)
    DMA("sync", idxs[:, :], idx_in[:, :], s_cst)
    N_CST = s_cst.n
    wt("vector", s_cst, N_CST)

    s_xp = [S("x0"), S("x1")]
    P1_TRB = 0
    wt("tensor", s_cst, N_CST)
    for t in range(16):
        if t >= 2:
            wt("sync", s_nrm, t - 1)
            wt("sync", s_sqa, t - 1)
        DMA("sync", xt[t % 2][:, :], x_in[t * 128:(t + 1) * 128, :], s_xp[t % 2])
        wt("vector", s_xp[t % 2], 16 * (t // 2 + 1))
        wt("scalar", s_xp[t % 2], 16 * (t // 2 + 1))
        RED(stats[:, t:t + 1], xt[t % 2][:, :], ADD, s_red)
        ACT(stg[t % 2][:, :], xt[t % 2][:, :], AF.Square,
            accum=stats[:, 16 + t:17 + t], sem=s_sqa)
        wt("vector", s_red, t + 1)
        wt("vector", s_sqa, t + 1)
        TS(stats[:, 32 + t:33 + t], stats[:, t:t + 1], -1.0 / D, None, MUL, None, sem=s_stt)
        wt("vector", s_stt, s_stt.n)
        TT("vector", stats[:, 64 + t:65 + t], stats[:, 32 + t:33 + t],
           stats[:, 32 + t:33 + t], MUL, s_stt)
        TS(stats[:, 48 + t:49 + t], stats[:, 16 + t:17 + t], 1.0 / D, EPS, MUL, ADD, sem=s_stt)
        wt("vector", s_stt, s_stt.n)
        TT("vector", stats[:, 48 + t:49 + t], stats[:, 48 + t:49 + t],
           stats[:, 64 + t:65 + t], AluOpType.subtract, s_stt)
        wt("vector", s_stt, s_stt.n)
        RCP(stats[:, 48 + t:49 + t], stats[:, 48 + t:49 + t], s_stt)
        wt("scalar", s_stt, s_stt.n)
        ACT(stats[:, 48 + t:49 + t], stats[:, 48 + t:49 + t], AF.Sqrt, sem=s_srt)
        wt("vector", s_srt, t + 1)
        if t >= 1:
            wt("vector", s_tr, 4 * t)
        TS(lnscr[:, :], xt[t % 2][:, :], stats[:, 32 + t:33 + t], stats[:, 48 + t:49 + t],
           ADD, MUL, sem=s_nrm)
        for j in range(4):
            half = (t * 4 + j) % 2
            wt("tensor", s_nrm, t + 1)
            if P1_TRB >= 2:
                wt("tensor", s_tre, P1_TRB - 1)
            for i in range(4):
                dc = 4 * j + i
                TR(psA[:, half * 512 + i * 128: half * 512 + (i + 1) * 128],
                   lnscr[:, dc * 128:(dc + 1) * 128],
                   sem=(s_tr if i == 3 else None))
            wt("vector", s_tr, s_tr.n)
            dst = xln1.rearrange("p (dt c) -> p dt c", c=T)[:, 4 * j:4 * j + 4, t * 128:(t + 1) * 128]
            CPY("vector", dst, psA.rearrange("p (q c) -> p q c", c=128)[:, half * 4:half * 4 + 4, :], s_tre)
            P1_TRB += 1

    def hbase(h):
        return (h % 2) * 3088 + (h // 2) * 386

    slices = []
    for h in list(range(0, H, 2)) + list(range(1, H, 2)):
        slices.append(("q", h))
        slices.append(("k", h))
        slices.append(("v", h))
    s_wldp = [S("wld0"), S("wld1")]
    s_st2p = [S("st0"), S("st1")]
    pend_norm = []
    vslice_cnt = 0
    nrmrow_cnt = 0

    def flush_norms():
        nonlocal nrmrow_cnt
        for fn in pend_norm:
            fn()
        pend_norm.clear()

    s_cc1e = S("cc1e")
    s_cc1o = S("cc1o")
    for si, (typ, h) in enumerate(slices):
        if si == 26:
            flush_norms()
            wt("gpsimd", s_st2p[0], s_st2p[0].n)
            wt("gpsimd", s_st2p[1], s_st2p[1].n)
            wt("gpsimd", s_nst, s_nst.n)
            run("gpsimd", lambda E: E.collective_compute(
                "AllGather", AluOpType.bypass, replica_groups=[list(range(NCORES))],
                ins=[kvq[0:3088, :].opt()], outs=[g1e.opt()]), s_cc1e, 1)
        col = {"q": 0, "k": D, "v": 2 * D}[typ] + h * 128
        if si >= 2:
            wt("gpsimd", s_qps, 4 * (si - 1))
        src = wqkvT[:, col:col + 128].rearrange("(dt p) e -> p dt e", p=128)
        dstw = wsb[si % 2][:, :].rearrange("p (dt e) -> p dt e", e=128)
        DMA("gpsimd", dstw, src, s_wldp[si % 2])
        wt("tensor", s_wldp[si % 2], 16 * (si // 2 + 1))
        if si == 0:
            wt("tensor", s_tre, P1_TRB)
            wt("tensor", s_cst, N_CST)
        for tc in range(4):
            if si > 0:
                wt("tensor", s_ev, 4 * (si - 1) + tc + 1)
            for dt in range(16):
                MM(psA[:, tc * 512:(tc + 1) * 512],
                   wsb[si % 2][:, dt * 128:(dt + 1) * 128],
                   xln1[:, dt * T + tc * 512: dt * T + (tc + 1) * 512],
                   start=(dt == 0), stop=(dt == 15),
                   sem=(s_qps if dt == 15 else None))
        flush_norms()
        if si >= 2:
            wt("vector", s_st2p[si % 2], 16 * ((si - 2) // 2 + 1))
        if typ == "v":
            wt("vector", s_vt, 4 * vslice_cnt)
        for tc in range(4):
            wt("vector", s_qps, 4 * si + tc + 1)
            qd = psA[:, tc * 512:(tc + 1) * 512]
            if typ == "q":
                TS(stg[si % 2][:, tc * 512:(tc + 1) * 512], qd,
                   bq[:, 0 + h:1 + h], ISC, ADD, MUL, sem=s_ev)
            elif typ == "k":
                TS(stg[si % 2][:, tc * 512:(tc + 1) * 512], qd,
                   bq[:, 16 + h:17 + h], None, ADD, sem=s_ev)
            else:
                TS(vstage[:, tc * 512:(tc + 1) * 512], qd,
                   bq[:, 32 + h:33 + h], None, ADD, sem=s_ev)
        if typ in ("q", "k"):
            rowbase = (hbase(h) + 256) if typ == "q" else hbase(h)
            wt("sync", s_ev, 4 * (si + 1))
            DMA("sync", kvq[rowbase:rowbase + 128, :], stg[si % 2][:, :], s_st2p[si % 2])
            par = (si // 3 * 2 + (0 if typ == "q" else 1)) % 2
            wt("vector", s_ev, 4 * (si + 1))
            if nrmrow_cnt >= 2:
                wt("vector", s_nq, 4 * (nrmrow_cnt - 1))
            TT("vector", q2scr[par][:, :], stg[si % 2][:, :], stg[si % 2][:, :], MUL, s_sqr)
            myrow = (hbase(h) + 385) if typ == "q" else (hbase(h) + 384)
            mycnt = nrmrow_cnt
            mysqr = s_sqr.n

            def mk_norm(par=par, myrow=myrow, mycnt=mycnt, mysqr=mysqr):
                wt("tensor", s_sqr, mysqr)
                for j in range(4):
                    bank = j % 2
                    if j >= 2:
                        wt("tensor", s_nev, 4 * mycnt + j - 1)
                    MM(psB[0:1, 512 * (1 + bank):512 * (1 + bank) + 512],
                       onec[:, :], q2scr[par][:, j * 512:(j + 1) * 512],
                       start=True, stop=True, sem=s_nq)
                for j in range(4):
                    bank = j % 2
                    wt("vector", s_nq, 4 * mycnt + j + 1)
                    CPY("vector", nrow[0:1, j * 512:(j + 1) * 512],
                        psB[0:1, 512 * (1 + bank):512 * (1 + bank) + 512], s_nev)
                wt("gpsimd", s_nev, 4 * (mycnt + 1))
                DMA("gpsimd", kvq[myrow:myrow + 1, :], nrow[0:1, :], s_nst)
            if nrmrow_cnt >= 1:
                def mk_guard(mc=nrmrow_cnt):
                    wt("vector", s_nst, 16 * mc)
                pend_norm.append(mk_guard)
            pend_norm.append(mk_norm)
            nrmrow_cnt += 1
        else:
            myev = s_ev.n
            wt("tensor", s_ev, myev)
            for j in range(4):
                if vslice_cnt * 4 + j >= 1:
                    wt("tensor", s_vte, vslice_cnt * 4 + j)
                for i in range(4):
                    kb = 4 * j + i
                    TR(psB[:, i * 128:(i + 1) * 128], vstage[:, kb * 128:(kb + 1) * 128],
                       sem=(s_vt if i == 3 else None))
                wt("vector", s_vt, s_vt.n)
                if j == 0 and si >= 2:
                    wt("vector", s_st2p[si % 2], 16 * ((si - 2) // 2 + 1))
                CPY("vector", stg[si % 2][:, j * 512:(j + 1) * 512], psB[:, 0:512], s_vte)
            vslice_cnt += 1
            wt("sync", s_vte, s_vte.n)
            DMA("sync", kvq[hbase(h) + 128: hbase(h) + 256, :], stg[si % 2][:, :], s_st2p[si % 2])
    flush_norms()
    wt("gpsimd", s_st2p[0], s_st2p[0].n)
    wt("gpsimd", s_st2p[1], s_st2p[1].n)
    wt("gpsimd", s_nst, s_nst.n)
    run("gpsimd", lambda E: E.collective_compute(
        "AllGather", AluOpType.bypass, replica_groups=[list(range(NCORES))],
        ins=[kvq[3088:6176, :].opt()], outs=[g1o.opt()]), s_cc1o, 1)

    if phase_stop <= 2:
        wt("sync", s_cc1e, 1)
        wt("sync", s_cc1o, 1)
        wt("sync", s_st2p[0], s_st2p[0].n)
        wt("sync", s_st2p[1], s_st2p[1].n)
        wt("sync", s_nst, s_nst.n)
        return finalize(nc, ops)

    s_cc2e = S("cc2e")
    s_cc2oa = S("cc2oa")
    s_cc2ob = S("cc2ob")
    s_pcc = [S("pcc0"), S("pcc1")]
    s_pst = [S("pst0"), S("pst1")]
    pc_tiles = []
    for wi, (wsrc_, wdst_, R, C) in enumerate((
            (woutT, woutb, D, D), (wgT, wgb, D, DH), (wuT, wub, D, DH),
            (wdT, wdb, DH, D))):
        for r0 in range(0, R, 256):
            for c0 in range(0, C, 2048):
                pc_tiles.append((wsrc_, wdst_, r0, c0, wi == 0))
    pc_state = {"i": 0, "store_done": []}

    def emit_precast(ntiles, cap=10**9):
        for _ in range(ntiles):
            i = pc_state["i"]
            if i >= len(pc_tiles):
                return
            pc_state["i"] = i + 1
            ws, wdst_, r0, c0, is_wout = pc_tiles[i]
            par_ = i % 2
            if i >= 8:
                thr = min(cap, int((i - 8) * TOTAL_PAIRS / 100))
                if thr > 0:
                    wt("gpsimd", s_av, thr)
            if i >= 2:
                gsem, gtgt = pc_state["store_done"][i - 2]
                wt("gpsimd", gsem, gtgt)
            DMA("gpsimd",
                pcst[par_][:, :].rearrange("p (t c) -> p t c", c=2048),
                ws[r0:r0 + 256, c0:c0 + 2048].rearrange("(t p) c -> p t c", p=128),
                s_pcc[par_])
            wt("gpsimd", s_pcc[par_], s_pcc[par_].n)
            DMA("gpsimd",
                wdst_[r0:r0 + 256, c0:c0 + 2048].rearrange("(t p) c -> p t c", p=128),
                pcst[par_][:, :].rearrange("p (t c) -> p t c", c=2048), s_pst[par_])
            pc_state["store_done"].append((s_pst[par_], s_pst[par_].n))

    pair_g = 0
    gq = 0
    pairs_done_slot = []

    wt("gpsimd", s_qps, s_qps.n)

    s_kvgp = [S("kvg0"), S("kvg1")]
    kvg_tgts = {}

    def emit_gather(slot):
        b = slot % 2
        ic = 13 * slot
        s_kvg = s_kvgp[b]
        g1v = g1e if slot < 2 else g1o
        wt("gpsimd", s_cc1e if slot < 2 else s_cc1o, 1)
        if slot >= 2:
            wt("gpsimd", s_av, pairs_done_slot[slot - 2])
        for i in range(4):
            IDMA(k_sb[b][:, i * 2048:(i + 1) * 2048], g1v, idxs[:, ic + i:ic + i + 1], 0, s_kvg)
        for i in range(4):
            IDMA(v_sb[b][:, i * 2048:(i + 1) * 2048], g1v, idxs[:, ic + 4 + i:ic + 5 + i], 0, s_kvg)
        for i in range(4):
            IDMA(q_sb[b][:, i * 2048:(i + 1) * 2048], g1v, idxs[:, ic + 8 + i:ic + 9 + i], 0, s_kvg)
        IDMA(kq2[:, b * 2048:(b + 1) * 2048], g1v, idxs[0:8, ic + 12:ic + 13], 0, s_kvg)
        kvg_tgts[slot] = s_kvg.n

    for slot in range(4):
        ss, par = [(0, 0), (1, 0), (0, 1), (1, 1)][slot]
        b = slot % 2
        ic = 13 * slot
        s_kvg = s_kvgp[b]
        g1v = g1e if par == 0 else g1o
        if slot == 0:
            emit_gather(0)
            emit_precast(8, cap=0)
            emit_gather(1)
            emit_precast(32, cap=PAIRS_PER_SLOT)
        elif slot == 2:
            emit_gather(2)
            wt("gpsimd", s_ast, 2 * 256)
            run("gpsimd", lambda E: E.collective_compute(
                "AllGather", AluOpType.bypass, replica_groups=[list(range(NCORES))],
                ins=[in2[0:256, :].opt()], outs=[g2e.opt()]), s_cc2e, 1)
            emit_precast(32, cap=2 * PAIRS_PER_SLOT)
            emit_gather(3)
            emit_precast(32, cap=3 * PAIRS_PER_SLOT)
            wt("gpsimd", s_ast, 3 * 256)
            run("gpsimd", lambda E: E.collective_compute(
                "AllGather", AluOpType.bypass, replica_groups=[list(range(NCORES))],
                ins=[in2[256:384, :].opt()], outs=[g2o[0:NCORES * 128, :].opt()]),
                s_cc2oa, 1)
        kvg_tgt = kvg_tgts[slot]
        wt("vector", s_kvg, kvg_tgt)
        RED(sml[0:8, 0:1], kq2[:, b * 2048:(b + 1) * 2048], AluOpType.max, s_mch)
        wt("sync", s_mch, s_mch.n)
        DMA("sync", nrm_d[0:1, 0:8], sml[0:8, 0:1], s_mch)
        wt("sync", s_mch, s_mch.n)
        DMA("sync", sml[0:1, 8:16], nrm_d[0:1, 0:8], s_mch)
        wt("vector", s_mch, s_mch.n)
        RED(sml[0:1, 0:1], sml[0:1, 8:12], AluOpType.max, s_mch)
        RED(sml[0:1, 1:2], sml[0:1, 12:16], AluOpType.max, s_mch)
        wt("vector", s_mch, s_mch.n)
        TS(sml[0:1, 2:3], sml[0:1, 0:1], sml[0:1, 1:2], 1e-20, MUL, ADD, sem=s_mch)
        wt("scalar", s_mch, s_mch.n)
        ACT(sml[0:1, 3:4], sml[0:1, 2:3], AF.Ln, sem=s_mch)
        wt("vector", s_mch, s_mch.n)
        TS(sml[0:1, 4:5], sml[0:1, 3:4], 0.5, None, MUL, None, sem=s_mch)
        wt("scalar", s_mch, s_mch.n)
        ACT(sml[0:1, 5:6], sml[0:1, 4:5], AF.Exp, sem=s_mch)
        wt("vector", s_mch, s_mch.n)
        TS(sml[0:1, 6:7], sml[0:1, 5:6], -1.0, None, MUL, None, sem=s_mch)
        wt("tensor", s_mch, s_mch.n)
        wt("tensor", s_ex, pair_g)
        MM(psA[:, 0:1], oner[:, :], sml[0:1, 6:7], start=True, stop=True, sem=s_bc)
        wt("vector", s_bc, s_bc.n)
        CPY("vector", negm[:, :], psA[:, 0:1], s_mch)
        negm_tgt = s_mch.n
        wt("tensor", s_kvg, kvg_tgt)
        wt("tensor", s_mch, negm_tgt)
        wt("scalar", s_mch, negm_tgt)
        wt("vector", s_kvg, kvg_tgt)
        if slot >= 2:
            wt("vector", s_ast, s_ast.n)
        deferred_div = []

        def flushdiv():
            for fn in deferred_div:
                fn()
            deferred_div.clear()

        plist = []
        for qs in range(16):
            npair = 2 * (qs + 1)
            for p in range(npair):
                plist.append((qs, p, npair))
        pair_g0 = pair_g
        gq0 = gq
        LEAD = 2
        post_pe = deferred_div

        def emit_S(idx, b=b):
            qs, p, npair = plist[idx]
            g = pair_g0 + idx
            stoff = (g % 2) * 1024
            if g >= 2:
                wt("tensor", s_ex, g - 1)
            for half in range(2):
                kb = 2 * p + half
                v = kb - 4 * qs
                masked = v >= 0
                MM(psA[:, stoff + half * 512: stoff + (half + 1) * 512],
                   k_sb[b][:, kb * 128:(kb + 1) * 128],
                   q_sb[b][:, qs * 512:(qs + 1) * 512],
                   start=True, stop=(not masked),
                   sem=(s_sq if (half == 1 and not masked) else None))
                if masked:
                    MM(psA[:, stoff + half * 512: stoff + (half + 1) * 512],
                       idnb[:, :], masks[:, v * 512:(v + 1) * 512],
                       start=False, stop=True,
                       sem=(s_sq if half == 1 else None))
            for f in post_pe:
                f()
            post_pe.clear()

        def emit_exp(idx):
            qs, p, npair = plist[idx]
            g = pair_g0 + idx
            stoff = (g % 2) * 1024
            wt("scalar", s_sq, g + 1)
            if g >= 3:
                wt("scalar", s_av, g - 3)
            ACT(pt[g % 4][:, :], psA[:, stoff:stoff + 1024], AF.Exp,
                bias=negm[:, 0:1], sem=s_ex)

        def emit_AV(idx, b=b, ss=ss, par=par):
            qs, p, npair = plist[idx]
            g = pair_g0 + idx
            gq_abs = gq0 + qs
            ob = qs % 2
            lb_l = 1024 + (qs % 2) * 512
            lb_b = 1024 + ((qs + 1) % 2) * 512
            wt("tensor", s_ex, g + 1)
            if p == 0:
                if gq_abs >= 2:
                    wt("tensor", s_div, gq_abs - 1)
                wt("tensor", s_lev, gq_abs)
            for half in range(2):
                kb = 2 * p + half
                MM(psB[:, ob * 512:(ob + 1) * 512],
                   v_sb[b][:, kb * 128:(kb + 1) * 128],
                   pt[g % 4][:, half * 512:(half + 1) * 512],
                   start=(p == 0 and half == 0), stop=(p == npair - 1 and half == 1))
            for half in range(2):
                MM(psB[0:1, lb_l:lb_l + 512], onec[:, :],
                   pt[g % 4][:, half * 512:(half + 1) * 512],
                   start=(p == 0 and half == 0), stop=(p == npair - 1 and half == 1),
                   sem=(s_av if half == 1 else None))
            if p == npair - 1:
                wt("vector", s_av, g + 1)
                RCP(linv[:, :], psB[0:1, lb_l:lb_l + 512], s_rcp)
                myrcp = s_rcp.n

                def mk_bcast(myrcp=myrcp, lb_b=lb_b):
                    wt("tensor", s_rcp, myrcp)
                    MM(psB[:, lb_b:lb_b + 512], oner[:, :], linv[:, :],
                       start=True, stop=True, sem=s_bc)
                post_pe.append(mk_bcast)
                wt("scalar", s_bc, s_bc.n + 1)
                ACT(linb[:, :], psB[:, lb_b:lb_b + 512], AF.Copy, sem=s_lev)
                wt("vector", s_lev, s_lev.n)
                TT("vector", ast[b][:, qs * 512:(qs + 1) * 512],
                   psB[:, ob * 512:(ob + 1) * 512], linb[:, :], MUL, s_div)
                wt("sync", s_div, s_div.n)
                DMA("sync", in2[(2 * par + ss) * 128:(2 * par + ss) * 128 + 128,
                                ss * SQ + qs * 512: ss * SQ + (qs + 1) * 512],
                    ast[b][:, qs * 512:(qs + 1) * 512], s_ast)

        NP = len(plist)
        for idx in range(NP):
            emit_S(idx)
            emit_exp(idx)
            if idx >= LEAD:
                emit_AV(idx - LEAD)
        for idx in range(NP - LEAD, NP):
            emit_AV(idx)
        pair_g += NP
        gq += 16
        for fn in deferred_div:
            fn()
        deferred_div.clear()
        pairs_done_slot.append(s_av.n)

    wt("gpsimd", s_ast, 4 * 256)
    run("gpsimd", lambda E: E.collective_compute(
        "AllGather", AluOpType.bypass, replica_groups=[list(range(NCORES))],
        ins=[in2[384:512, :].opt()], outs=[g2o[NCORES * 128:, :].opt()]), s_cc2ob, 1)

    if phase_stop <= 4:
        wt("sync", s_cc2e, 1)
        wt("sync", s_cc2oa, 1)
        wt("sync", s_cc2ob, 1)
        wt("sync", s_ast, s_ast.n)
        return finalize(nc, ops)

    g2ev = g2e.rearrange("r (blk t) -> r blk t", t=2048)
    g2ov = g2o.rearrange("r (blk t) -> r blk t", t=2048)
    wt("gpsimd", s_cc2e, 1)
    wt("gpsimd", s_div, s_div.n)
    for dt in range(0, 16, 2):
        IDMA(afm[:, dt * T:(dt + 1) * T], g2ev, idxs[:, 52 + dt:53 + dt], 1, s_pg)
    wt("gpsimd", s_cc2oa, 1)
    wt("gpsimd", s_cc2ob, 1)
    for dt in range(1, 16, 2):
        IDMA(afm[:, dt * T:(dt + 1) * T], g2ov, idxs[:, 52 + dt:53 + dt], 1, s_pg)
    emit_precast(len(pc_tiles))
    s_xrp = [S("xr0"), S("xr1")]
    for tc0 in range(2):
        wt("sync", s_av, s_av.n)
        DMA("sync", xres[tc0][:, :], x_in[tc0 * 128:(tc0 + 1) * 128, :], s_xrp[tc0])
    for ssem_, stgt_ in pc_state["store_done"][6:8]:
        wt("sync", ssem_, stgt_)
    DMA("sync", wout[:, :].rearrange("p (dt e) -> p dt e", e=D),
        woutb.rearrange("(dt p) e -> p dt e", p=128), s_pg)
    pg_tgt = s_pg.n
    wt("tensor", s_ex, s_ex.n)
    wt("tensor", s_lev, s_lev.n)
    wt("tensor", s_div, s_div.n)
    for j in range(4):
        MM(psB[:, j * 512:(j + 1) * 512], oner[:, :], bor[0:1, j * 512:(j + 1) * 512],
           start=True, stop=True, sem=s_bcb)
    wt("scalar", s_bcb, s_bcb.n)
    wt("scalar", s_pg, 16)
    ACT(bob[:, :], psB[:, :], AF.Copy, sem=s_bcb)
    bcb_tgt = s_bcb.n
    wt("tensor", s_pg, pg_tgt)
    wt("tensor", s_bcb, bcb_tgt)
    wt("vector", s_bcb, bcb_tgt)
    for tc in range(16):
        ps = psA if tc % 2 == 0 else psB
        if tc >= 2:
            wt("sync", s_a2, tc - 1)
            DMA("sync", xres[tc % 2][:, :], x_in[tc * 128:(tc + 1) * 128, :], s_xrp[tc % 2])
        if tc >= 2:
            wt("tensor", s_a1, tc - 1)
        for e4 in range(4):
            for dt in range(16):
                MM(ps[:, e4 * 512:(e4 + 1) * 512],
                   afm[:, dt * T + tc * 128: dt * T + tc * 128 + 128],
                   wout[:, dt * D + e4 * 512: dt * D + (e4 + 1) * 512],
                   start=(dt == 0), stop=(dt == 15),
                   sem=(s_y if e4 == 3 and dt == 15 else None))
        wt("vector", s_y, tc + 1)
        if tc >= 1:
            wt("vector", s_x2s, 16 * tc)
        TT("vector", x2st[:, :], ps[:, :], bob[:, :], ADD, s_a1)
        wt("vector", s_xrp[tc % 2], 16 * (tc // 2 + 1))
        wt("vector", s_a1, s_a1.n)
        TT("vector", x2st[:, :], x2st[:, :], xres[tc % 2][:, :], ADD, s_a2)
        wt("sync", s_a2, s_a2.n)
        DMA("sync", x2b[tc * 128:(tc + 1) * 128, :], x2st[:, :], s_x2s)

    if phase_stop <= 5:
        wt("sync", s_x2s, s_x2s.n)
        finish(ops)
        return finalize(nc, ops)

    s_dhc = S("dhc")
    s_gus = [S(f"gus{r}") for r in range(4)]
    s_dws = [S(f"dws{r}") for r in range(2)]
    gu_rel = [0, 0, 0, 0]
    dchunks = [(hf_, sup_, e4_, hc_) for hf_ in range(2) for sup_ in range(4)
               for e4_ in range(4) for hc_ in range(16)]
    NDCH = len(dchunks)
    dptr = [0]
    demit = [0]
    dload = {}
    dcomp = {}
    for hf in range(2):
        tok0 = hf * 1024
        if hf == 0:
            wt("sync", s_a2, s_a2.n)
            wt("sync", s_y, s_y.n)
        for tt in range(8):
            wt("sync", s_x2s, 256)
            if hf == 1 and tt == 0:
                wt("sync", s_ye, 128)
                wt("sync", s_out, 16 * 8)
            DMA("sync", y2[:, tt * D:(tt + 1) * D],
                x2b[(tok0 // 128 + tt) * 128:(tok0 // 128 + tt + 1) * 128, :], s_x2l)
        if "p6s" in dbg and hf == 1:
            wt("vector", s_cst, dbg_dump_tgt)
            wt("scalar", s_cst, dbg_dump_tgt)
        red0, sqa0 = s_red.n, s_sqa.n
        wt("vector", s_x2l, 16 * 8 * (hf + 1))
        wt("scalar", s_x2l, 16 * 8 * (hf + 1))
        if hf == 0:
            wt("scalar", s_pst[0], s_pst[0].n)
            wt("scalar", s_pst[1], s_pst[1].n)
        for tt in range(8):
            if tt >= 1:
                wt("scalar", s_sqa, sqa0 + tt)
            RED(stats[:, tt:tt + 1], y2[:, tt * D:(tt + 1) * D], ADD, s_red)
            ACT(lnscr2[:, :], y2[:, tt * D:(tt + 1) * D], AF.Square,
                accum=stats[:, 16 + tt:17 + tt], sem=s_sqa)
        wt("vector", s_red, red0 + 8)
        wt("vector", s_sqa, sqa0 + 8)
        TS(stats[:, 32:40], stats[:, 0:8], -1.0 / D, None, MUL, None, sem=s_stt)
        wt("vector", s_stt, s_stt.n)
        TT("vector", stats[:, 64:72], stats[:, 32:40], stats[:, 32:40], MUL, s_stt)
        TS(stats[:, 48:56], stats[:, 16:24], 1.0 / D, EPS, MUL, ADD, sem=s_stt)
        wt("vector", s_stt, s_stt.n)
        TT("vector", stats[:, 48:56], stats[:, 48:56], stats[:, 64:72], AluOpType.subtract, s_stt)
        wt("vector", s_stt, s_stt.n)
        RCP(stats[:, 48:56], stats[:, 48:56], s_stt)
        wt("scalar", s_stt, s_stt.n)
        ACT(stats[:, 48:56], stats[:, 48:56], AF.Sqrt, sem=s_srt)
        srt_tgt = s_srt.n
        trb0 = s_tr.n
        nrm0 = s_nrm.n
        tre0 = s_tre.n
        for tt in range(8):
            wt("vector", s_srt, srt_tgt)
            wt("vector", s_sqa, sqa0 + 8)
            if tt >= 1:
                wt("vector", s_tr, trb0 + 4 * tt)
            TS(lnscr2[:, :], y2[:, tt * D:(tt + 1) * D],
               stats[:, 32 + tt:33 + tt], stats[:, 48 + tt:49 + tt], ADD, MUL, sem=s_nrm)
            for j in range(4):
                half = (tt * 4 + j) % 2
                wt("tensor", s_nrm, nrm0 + tt + 1)
                if tt == 0 and j == 0:
                    wt("tensor", s_a1, s_a1.n)
                    wt("tensor", s_x2l, 16 * (8 * hf + 8))
                    wt("tensor", s_ye, s_ye.n)
                if (tt * 4 + j) >= 1:
                    wt("tensor", s_tre, tre0 + (tt * 4 + j))
                for i in range(4):
                    dc = 4 * j + i
                    TR(psB[:, half * 512 + i * 128: half * 512 + (i + 1) * 128],
                       lnscr2[:, dc * 128:(dc + 1) * 128],
                       sem=(s_tr if i == 3 else None))
                wt("vector", s_tr, s_tr.n)
                dst = xln2.rearrange("p (dt c) -> p dt c", c=1024)[:, 4 * j:4 * j + 4, tt * 128:(tt + 1) * 128]
                CPY("vector", dst, psB.rearrange("p (q c) -> p q c", c=128)[:, half * 4:half * 4 + 4, :], s_tre)
        xln2_tre = s_tre.n
        if "p6s" in dbg and hf == 0:
            dbg_st2 = nc.dram_tensor("dbg_st2", [128, 80], F32, kind="ExternalOutput").ap()
            dbg_y2 = nc.dram_tensor("dbg_y2", [128, 8 * D], F32, kind="ExternalOutput").ap()
            wt("sync", s_srt, s_srt.n)
            DMA("sync", dbg_st2[:, :], stats[:, :], s_cst)
            wt("sync", s_x2l, 16 * 8)
            DMA("sync", dbg_y2[:, :], y2[:, :], s_cst)
            dbg_dump_tgt = s_cst.n
        if "p6i" in dbg and hf == 0:
            dbg_xln2 = nc.dram_tensor("dbg_xln2", [128, 16 * 1024], BF16, kind="ExternalOutput").ap()
            wt("sync", s_tre, xln2_tre)
            DMA("sync", dbg_xln2[:, :], xln2[:, :], s_cst)
        for sup in range(4):
            gmm0 = s_gmm.n
            next_load = 0
            gld_done = {}

            def emit_gu(j2):
                nonlocal next_load
                ring = j2 % 2
                wt("scalar", s_gmm, gu_rel[ring])
                if pc_state["i"] and gu_rel[ring] == 0:
                    wt("scalar", s_pst[0], s_pst[0].n)
                    wt("scalar", s_pst[1], s_pst[1].n)
                h0 = sup * 2048 + j2 * 256
                tgts = []
                for gi, wsrc in ((0, wgb), (1, wub)):
                    DMA("scalar",
                        gu[ring][:, gi * 4096:(gi + 1) * 4096].rearrange("p (dt e) -> p dt e", e=256),
                        wsrc[:, h0:h0 + 256].rearrange("(dt p) e -> p dt e", p=128),
                        s_gus[ring])
                    tgts.append((s_gus[ring], s_gus[ring].n))
                gu_rel[ring] = gmm0 + 4 * (2 * j2 + 1) + 4
                gld_done[j2] = tgts
                next_load = j2 + 1

            for hc in range(16):
                while next_load <= min(hc // 2 + 1, 7):
                    emit_gu(next_load)
                j2 = hc // 2
                for gsem_, gtgt_ in gld_done[j2]:
                    wt("tensor", gsem_, gtgt_)
                if sup == 0 and hc == 0:
                    wt("tensor", s_tre, xln2_tre)
                wt("tensor", s_sil, s_sil.n)
                wt("tensor", s_gtd, s_gtd.n)
                if hc == 0:
                    wt("tensor", s_ye, s_ye.n)
                goff = (hc % 2) * 128
                for ts in range(2):
                    for dt in range(16):
                        MM(psA[:, ts * 512:(ts + 1) * 512],
                           gu[j2 % 2][:, dt * 256 + goff: dt * 256 + goff + 128],
                           xln2[:, dt * 1024 + ts * 512: dt * 1024 + (ts + 1) * 512],
                           start=(dt == 0), stop=(dt == 15),
                           sem=(s_gmm if dt == 15 else None))
                for ts in range(2):
                    for dt in range(16):
                        MM(psA[:, 1024 + ts * 512: 1024 + (ts + 1) * 512],
                           gu[j2 % 2][:, 4096 + dt * 256 + goff: 4096 + dt * 256 + goff + 128],
                           xln2[:, dt * 1024 + ts * 512: dt * 1024 + (ts + 1) * 512],
                           start=(dt == 0), stop=(dt == 15),
                           sem=(s_gmm if dt == 15 else None))
                wt("scalar", s_gmm, gmm0 + hc * 4 + 2)
                wt("scalar", s_gtd, s_gtd.n)
                ACT(lnscr2[:, 0:1024], psA[:, 0:1024], AF.Silu, sem=s_sil)
                wt("vector", s_gmm, gmm0 + hc * 4 + 4)
                wt("vector", s_sil, s_sil.n)
                TT("vector", gated[:, hc * 1024:(hc + 1) * 1024],
                   lnscr2[:, 0:1024], psA[:, 1024:2048], MUL, s_gtd)
            gtd_tgt = s_gtd.n
            if "p6i" in dbg and hf == 0 and sup == 0:
                dbg_gtd = nc.dram_tensor("dbg_gtd", [128, 16 * 1024], BF16, kind="ExternalOutput").ap()
                wt("sync", s_gtd, gtd_tgt)
                DMA("sync", dbg_gtd[:, :], gated[:, :], s_cst)
            for e4 in range(4):
                for hc in range(16):
                    n = dptr[0] + e4 * 16 + hc
                    while demit[0] <= min(n + 1, NDCH - 1):
                        m = demit[0]
                        mhf, msup, me4, mhc = dchunks[m]
                        if m >= 2:
                            gsem, gtgt = dcomp[m - 2]
                            wt("scalar", gsem, gtgt)
                        else:
                            wt("scalar", s_pst[0], s_pst[0].n)
                            wt("scalar", s_pst[1], s_pst[1].n)
                        DMA("scalar", dw[m % 2][:, :],
                            wdb[(msup * 16 + mhc) * 128:(msup * 16 + mhc + 1) * 128,
                                me4 * 512:(me4 + 1) * 512], s_dws[m % 2])
                        dload[m] = (s_dws[m % 2], s_dws[m % 2].n)
                        demit[0] += 1
                    wt("tensor", dload[n][0], dload[n][1])
                    if e4 == 0:
                        wt("tensor", s_gtd, gtd_tgt - 16 + hc + 1)
                        if hc == 0:
                            wt("tensor", s_sil, s_sil.n)
                            wt("tensor", s_ye, s_ye.n)
                    elif hc == 0:
                        wt("tensor", s_ye, s_ye.n)
                    for tc in range(8):
                        ps = psB if tc < 4 else psA
                        qd = tc % 4
                        sem = None
                        if hc == 15:
                            sem = s_dmm
                        elif tc == 7:
                            sem = s_dhc
                        MM(ps[:, qd * 512:(qd + 1) * 512],
                           gated[:, hc * 1024 + tc * 128: hc * 1024 + tc * 128 + 128],
                           dw[n % 2][:, :],
                           start=(hc == 0), stop=(hc == 15), sem=sem)
                    dcomp[n] = (s_dhc, s_dhc.n) if hc < 15 else (s_dmm, s_dmm.n)
                for tc in range(8):
                    ps = psB if tc < 4 else psA
                    qd = tc % 4
                    wt("vector", s_dmm, s_dmm.n - 8 + tc + 1)
                    TT("vector", y2[:, tc * D + e4 * 512: tc * D + (e4 + 1) * 512],
                       y2[:, tc * D + e4 * 512: tc * D + (e4 + 1) * 512],
                       ps[:, qd * 512:(qd + 1) * 512], ADD, s_ye)
            dptr[0] += 64
        ye_end = s_ye.n
        for tt in range(8):
            wt("sync", s_ye, ye_end - 8 + tt + 1)
            DMA("sync", out_d[(tok0 // 128 + tt) * 128:(tok0 // 128 + tt + 1) * 128, :],
                y2[:, tt * D:(tt + 1) * D], s_out)

    wt("sync", s_out, s_out.n)
    return finalize(nc, ops)


def finish(ops):
    pass


def finalize(nc, ops):
    blk = nc.Block().__enter__()
    for name in ("tensor", "vector", "scalar", "gpsimd", "sync"):
        lst = ops[name]

        def runner(E, lst=lst):
            for f in lst:
                f(E)
        getattr(blk, name)(runner)
    blk.__exit__(None, None, None)
    return nc


def _prep_inputs(L0_attn_x, L0_attn_w_qkv, L0_attn_b_qkv, L0_attn_w_out,
                 L0_attn_b_out, L0_ffn_w_gate, L0_ffn_w_up, L0_ffn_w_down):
    x = np.asarray(L0_attn_x, dtype=np.float32)
    wqkvT = np.ascontiguousarray(np.asarray(L0_attn_w_qkv, dtype=np.float32).T)
    woutT = np.ascontiguousarray(np.asarray(L0_attn_w_out, dtype=np.float32).T)
    wgT = np.ascontiguousarray(np.asarray(L0_ffn_w_gate, dtype=np.float32).T)
    wuT = np.ascontiguousarray(np.asarray(L0_ffn_w_up, dtype=np.float32).T)
    wdT = np.ascontiguousarray(np.asarray(L0_ffn_w_down, dtype=np.float32).T)
    bq = np.ascontiguousarray(
        np.asarray(L0_attn_b_qkv, dtype=np.float32).reshape(48, 128).T)
    bor = np.ascontiguousarray(np.asarray(L0_attn_b_out, dtype=np.float32)[None, :])
    masks = np.zeros((128, 2048), dtype=np.float32)
    ki = np.arange(128)[:, None]
    qj = np.arange(512)[None, :]
    for v in range(4):
        masks[:, v * 512:(v + 1) * 512] = np.where(v * 128 + ki <= qj, 0.0, NEGBIG)
    masks = masks.astype(ml_dtypes.bfloat16)
    idn = np.eye(128, dtype=np.float32)
    idnb = np.eye(128, dtype=ml_dtypes.bfloat16)
    onec = np.ones((128, 1), dtype=ml_dtypes.bfloat16)
    oner = np.ones((1, 128), dtype=np.float32)

    shared = dict(wqkvT=wqkvT, woutT=woutT, wgT=wgT, wuT=wuT, wdT=wdT,
                  bqd=bq, bord=bor, masksd=masks, idnd=idn, idnbd=idnb,
                  onecd=onec, onerd=oner)
    in_maps = []
    for c in range(NCORES):
        s = c // 4
        q0 = (c % 4) * T
        idx = np.zeros((128, 72), dtype=np.uint32)
        ar = np.arange(128, dtype=np.uint32)
        for slot in range(4):
            ss, par = [(0, 0), (1, 0), (0, 1), (1, 1)][slot]
            h = 2 * c + par
            ic = 13 * slot
            hb = (h // 2) * 386
            for i in range(4):
                prow = (ss * 4 + i) * 3088
                idx[:, ic + i] = prow + hb + ar
                idx[:, ic + 4 + i] = prow + hb + 128 + ar
                idx[:, ic + 8 + i] = prow + hb + 256 + ar
                idx[i, ic + 12] = prow + hb + 384
                idx[4 + i, ic + 12] = prow + hb + 385
        for dt in range(16):
            hh = dt
            if hh % 2 == 0:
                rowbase = (hh // 2) * 256 + s * 128
            else:
                rowbase = s * 1024 + (hh // 2) * 128
            colblk = s * 4 + (c % 4)
            idx[:, 52 + dt] = (rowbase + ar) * 8 + colblk
        m = dict(shared)
        m["x"] = np.ascontiguousarray(x[s, q0:q0 + T, :])
        m["idxd"] = idx
        in_maps.append(m)
    return in_maps


def kernel(**inputs):
    in_maps = _prep_inputs(**inputs)
    nc = build()
    res = run_bass_kernel_spmd(nc, in_maps, core_ids=list(range(NCORES)))
    out = np.zeros((2, SQ, D), dtype=np.float32)
    for c in range(NCORES):
        s = c // 4
        q0 = (c % 4) * T
        out[s, q0:q0 + T, :] = np.asarray(res.results[c]["out"])
    return out


# version 23
# speedup vs baseline: 1.0252x; 1.0252x over previous
import math
import numpy as np
import ml_dtypes
import concourse.bass as bass
import concourse.mybir as mybir
from concourse.bass import IndirectOffsetOnAxis
from concourse.bass_utils import run_bass_kernel_spmd
from concourse.alu_op_type import AluOpType
from concourse.replica_groups import maybe_share_collective_output_space

F32 = mybir.dt.float32
BF16 = mybir.dt.bfloat16
U32 = mybir.dt.uint32
AF = mybir.ActivationFunctionType
AX = mybir.AxisListType
ADD = AluOpType.add
MUL = AluOpType.mult

NCORES = 8
D = 2048
H = 16
HD = 128
T = 2048
SQ = 8192
DH = 8192
E3 = 3 * D
ISC = 1.0 / math.sqrt(HD)
EPS = 1e-5
NEGBIG = -30000.0
KVQ_ROWS = 3 * D + 32
PAIRS_PER_SLOT = sum(2 * (qs + 1) for qs in range(16))
TOTAL_PAIRS = 4 * PAIRS_PER_SLOT


class Sem:
    def __init__(self, h):
        self.h = h
        self.n = 0

    def bump(self, k=1):
        self.n += k
        return self.n


def build(dbg=(), phase_stop=99):
    nc = bass.Bass("TRN2", target_bir_lowering=False, debug=False)

    def kind_of(name):
        return "ExternalOutput" if name in dbg else "Internal"

    x_in = nc.dram_tensor("x", [T, D], F32, kind="ExternalInput").ap()
    wqkvT = nc.dram_tensor("wqkvT", [D, E3], F32, kind="ExternalInput").ap()
    woutT = nc.dram_tensor("woutT", [D, D], F32, kind="ExternalInput").ap()
    wgT = nc.dram_tensor("wgT", [D, DH], F32, kind="ExternalInput").ap()
    wuT = nc.dram_tensor("wuT", [D, DH], F32, kind="ExternalInput").ap()
    wdT = nc.dram_tensor("wdT", [DH, D], F32, kind="ExternalInput").ap()
    bq_in = nc.dram_tensor("bqd", [128, 48], F32, kind="ExternalInput").ap()
    bor_in = nc.dram_tensor("bord", [1, D], F32, kind="ExternalInput").ap()
    masks_in = nc.dram_tensor("masksd", [128, 2048], BF16, kind="ExternalInput").ap()
    idnb_in = nc.dram_tensor("idnbd", [128, 128], BF16, kind="ExternalInput").ap()
    idn_in = nc.dram_tensor("idnd", [128, 128], F32, kind="ExternalInput").ap()
    onec_in = nc.dram_tensor("onecd", [128, 1], BF16, kind="ExternalInput").ap()
    oner_in = nc.dram_tensor("onerd", [1, 128], F32, kind="ExternalInput").ap()
    idx_in = nc.dram_tensor("idxd", [128, 72], U32, kind="ExternalInput").ap()
    out_d = nc.dram_tensor("out", [T, D], F32, kind="ExternalOutput").ap()

    space = maybe_share_collective_output_space("AllGather", [list(range(NCORES))])
    kvq_t = nc.dram_tensor("kvq_in", [KVQ_ROWS, D], BF16, kind=kind_of("kvq_in"))
    g1e_t = nc.dram_tensor("g1e", [NCORES * 3088, D], BF16, kind="Internal",
                           addr_space=space)
    g1o_t = nc.dram_tensor("g1o", [NCORES * 3088, D], BF16, kind="Internal",
                           addr_space=space)
    in2_t = nc.dram_tensor("in2", [512, 2 * SQ], BF16, kind=kind_of("in2"))
    g2e_t = nc.dram_tensor("g2e", [NCORES * 256, 2 * SQ], BF16, kind="Internal",
                           addr_space=space)
    g2o_t = nc.dram_tensor("g2o", [NCORES * 256, 2 * SQ], BF16, kind="Internal",
                           addr_space=space)
    x2b_t = nc.dram_tensor("x2b", [T, D], F32, kind=kind_of("x2b"))
    woutb = nc.dram_tensor("woutb", [D, D], BF16, kind="Internal").ap()
    wgb = nc.dram_tensor("wgb", [D, DH], BF16, kind="Internal").ap()
    wub = nc.dram_tensor("wub", [D, DH], BF16, kind="Internal").ap()
    wdb = nc.dram_tensor("wdb", [DH, D], BF16, kind="Internal").ap()
    nrm_t = nc.dram_tensor("nrm_d", [1, 64], F32, kind="Internal")
    kvq = kvq_t.ap()
    g1e = g1e_t.ap()
    g1o = g1o_t.ap()
    in2 = in2_t.ap()
    g2e = g2e_t.ap()
    g2o = g2o_t.ap()
    x2b = x2b_t.ap()
    nrm_d = nrm_t.ap()

    SB0 = 16512
    def sbt(name, shape, dtype, off):
        off = off + SB0
        assert off + shape[-1] * {F32: 4, BF16: 2, U32: 4}[dtype] * (
            1 if len(shape) == 2 else shape[1]) <= 229344, name
        return nc.alloc_sbuf_tensor_at(name, shape, dtype, offset=off).ap()

    masks = sbt("masks", [128, 2048], BF16, 0)
    idnb = sbt("idnb", [128, 128], BF16, 4096)
    bob = sbt("bob", [128, 2048], F32, 8192)
    idn = sbt("idn", [128, 128], F32, 16384)
    onec = sbt("onec", [128, 1], BF16, 16896)
    oner = sbt("oner", [1, 128], F32, 16928)
    bq = sbt("bq", [128, 48], F32, 17440)
    bqs = sbt("bqs", [128, 16], F32, 17664)
    bor = sbt("bor", [1, D], F32, 17728)
    idxs = sbt("idxs", [128, 72], U32, 25920)
    stats = sbt("stats", [128, 80], F32, 26240)
    nrow = sbt("nrow", [1, 2048], F32, 26560)
    sml = sbt("sml", [8, 16], F32, 34752)
    BASE = 34848

    xln1 = sbt("xln1", [128, 16 * T], BF16, BASE)
    wsb = ([sbt(f"wsb{i}", [128, 2048], BF16, BASE + 65536 + 4096 * i) for i in range(2)]
           + [sbt(f"wsb{i + 2}", [128, 2048], BF16, BASE + 122880 + 4096 * i) for i in range(2)])
    stg = [sbt(f"stg{i}", [128, 2048], BF16, BASE + 73728 + 4096 * i) for i in range(2)]
    vstage = sbt("vstage", [128, 2048], F32, BASE + 81920)
    q2scr = [sbt(f"q2s{i}", [128, 2048], BF16, BASE + 90112 + 4096 * i) for i in range(2)]
    xt = [sbt(f"xt{i}", [128, 2048], F32, BASE + 98304 + 8192 * i) for i in range(2)]
    lnscr = sbt("lnscr", [128, 2048], F32, BASE + 114688)

    k_sb = [sbt(f"ksb{i}", [128, SQ], BF16, BASE + 16384 * i) for i in range(2)]
    v_sb = [sbt(f"vsb{i}", [128, SQ], BF16, BASE + 32768 + 16384 * i) for i in range(2)]
    q_sb = [sbt(f"qsb{i}", [128, SQ], BF16, BASE + 65536 + 16384 * i) for i in range(2)]
    ast = [sbt(f"ast{i}", [128, SQ], BF16, BASE + 98304 + 16384 * i) for i in range(2)]
    pt = [sbt(f"pt{i}", [128, 1024], BF16, BASE + 131072 + 2048 * i) for i in range(4)]
    linb = sbt("linb", [128, 512], F32, BASE + 139264)
    kq2 = sbt("kq2", [8, 2 * 2048], BF16, BASE + 141312)
    negm = sbt("negm", [128, 1], F32, BASE + 149504)
    linv = sbt("linv", [1, 512], F32, 26560 - 16512)

    afm = sbt("afm", [128, 16 * T], BF16, BASE)
    wout = sbt("wout", [128, 16 * D], BF16, BASE + 65536)
    xres = [sbt(f"xres{i}", [128, 2048], F32, BASE + 131072 + 8192 * i) for i in range(2)]
    x2st = sbt("x2st", [128, 2048], F32, BASE + 147456)

    y2 = sbt("y2", [128, 8 * D], F32, BASE)
    xln2 = sbt("xln2", [128, 16 * 1024], BF16, BASE + 65536)
    gated = sbt("gated", [128, 16 * 1024], BF16, BASE + 98304)
    gu = [sbt(f"gu{i}", [128, 8192], BF16, BASE + 131072 + 16384 * i) for i in range(2)]
    pcst = [sbt(f"pcst{i}", [128, 4096], BF16, BASE + 155648 + 8192 * i) for i in range(2)]
    lnscr2 = sbt("lnscr2", [128, 2048], F32, BASE + 163840)
    dw = [sbt(f"dw{i}", [128, 512], BF16, BASE + 172032 + 1024 * i) for i in range(2)]

    psA = nc.alloc_psum_tensor("psA", [128, 2048], F32).ap()
    psB = nc.alloc_psum_tensor("psB", [128, 2048], F32).ap()

    ops = {k: [] for k in ("tensor", "vector", "scalar", "gpsimd", "sync")}
    sems = {}

    def S(name):
        if name not in sems:
            sems[name] = Sem(nc.semaphore(name).__enter__())
        return sems[name]

    def em(eng, fn):
        ops[eng].append(fn)

    def wt(eng, sem, v):
        if v > 0:
            em(eng, lambda E, h=sem.h, vv=v: E.wait_ge(h, vv))

    def run(eng, fn, sem=None, k=1):
        if sem is None:
            em(eng, fn)
        else:
            sem.bump(k)
            em(eng, lambda E, f=fn, h=sem.h, kk=k: f(E).then_inc(h, kk))

    def MM(out, lhsT, rhs, start, stop, sem=None, k=1):
        run("tensor", lambda E, o=out, l=lhsT, r=rhs, s1=start, s2=stop:
            E.matmul(o, l, r, start=s1, stop=s2), sem, k)

    def TR(out, in_, sem=None, k=1):
        run("tensor", lambda E, o=out, i=in_: E.transpose(o, i, idn), sem, k)

    def ACT(out, in_, func, bias=0.0, scale=1.0, accum=None, sem=None, k=1):
        run("scalar", lambda E, o=out, i=in_, f=func, b=bias, s=scale, a=accum:
            E.activation(o, i, f, bias=b, scale=s, accum_out=a), sem, k)

    def TT(eng, out, in0, in1, op, sem=None, k=1):
        run(eng, lambda E, o=out, a=in0, b=in1, p=op: E.tensor_tensor(o, a, b, p), sem, k)

    def TS(out, in0, s1, s2, op0, op1=None, accum=None, sem=None, k=1):
        if op1 is None:
            run("vector", lambda E, o=out, i=in0, a=s1, p0=op0:
                E.tensor_scalar(o, i, a, None, p0), sem, k)
        else:
            run("vector", lambda E, o=out, i=in0, a=s1, b=s2, p0=op0, p1=op1, ac=accum:
                E.tensor_scalar(o, i, a, b, p0, p1, accum_out=ac), sem, k)

    def RED(out, in_, op, sem=None, k=1):
        run("vector", lambda E, o=out, i=in_, p=op: E.tensor_reduce(o, i, AX.X, p), sem, k)

    def CPY(eng, out, in_, sem=None, k=1):
        run(eng, lambda E, o=out, i=in_: E.tensor_copy(o, i), sem, k)

    def RCP(out, in_, sem=None, k=1):
        run("vector", lambda E, o=out, i=in_: E.reciprocal(o, i), sem, k)

    def DMA(eng, out, in_, sem=None):
        run(eng, lambda E, o=out, i=in_: E.dma_start(out=o, in_=i), sem, 16)

    def IDMA(out, src, idx_ap, axis, sem=None):
        run("gpsimd", lambda E, o=out, s=src, ia=idx_ap, ax=axis:
            E.indirect_dma_start(out=o, out_offset=None, in_=s,
                                 in_offset=IndirectOffsetOnAxis(ia, ax)), sem, 16)

    s_cst = S("cst")
    s_x = S("x")
    s_red = S("red")
    s_sqa = S("sqa")
    s_stt = S("stt")
    s_srt = S("srt")
    s_nrm = S("nrm")
    s_tr = S("tr")
    s_tre = S("tre")
    s_wld = S("wld")
    s_qps = S("qps")
    s_ev = S("ev")
    s_st2 = S("st2")
    s_sqr = S("sqr")
    s_nq = S("nq")
    s_nev = S("nev")
    s_nst = S("nst")
    s_vt = S("vt")
    s_vte = S("vte")
    s_cc1 = S("cc1")
    s_kvg = S("kvg")
    s_mch = S("mch")
    s_sq = S("sq")
    s_msk = S("msk")
    s_ex = S("ex")
    s_av = S("av")
    s_rcp = S("rcp")
    s_bc = S("bc")
    s_lev = S("lev")
    s_div = S("div")
    s_ast = S("ast")
    s_cc2 = S("cc2")
    s_pg = S("pg")
    s_bcb = S("bcb")
    s_y = S("y")
    s_a1 = S("a1")
    s_a2 = S("a2")
    s_xr = S("xr")
    s_x2s = S("x2s")
    s_x2l = S("x2l")
    s_gld = S("gld")
    s_gmm = S("gmm")
    s_sil = S("sil")
    s_gtd = S("gtd")
    s_dld = S("dld")
    s_dmm = S("dmm")
    s_ye = S("ye")
    s_out = S("out")

    DMA("sync", masks[:, :], masks_in[:, :], s_cst)
    DMA("sync", idnb[:, :], idnb_in[:, :], s_cst)
    DMA("sync", idn[:, :], idn_in[:, :], s_cst)
    DMA("sync", onec[:, :], onec_in[:, :], s_cst)
    DMA("sync", oner[:, :], oner_in[:, :], s_cst)
    DMA("sync", bq[:, :], bq_in[:, :], s_cst)
    DMA("sync", bor[:, :], bor_in[:, :], s_cst)
    DMA("sync", idxs[:, :], idx_in[:, :], s_cst)
    N_CST = s_cst.n
    wt("vector", s_cst, N_CST)

    s_xp = [S("x0"), S("x1")]
    P1_TRB = 0
    wt("tensor", s_cst, N_CST)
    for t in range(16):
        if t >= 2:
            wt("sync", s_nrm, t - 1)
            wt("sync", s_sqa, t - 1)
        DMA("sync", xt[t % 2][:, :], x_in[t * 128:(t + 1) * 128, :], s_xp[t % 2])
        wt("vector", s_xp[t % 2], 16 * (t // 2 + 1))
        wt("scalar", s_xp[t % 2], 16 * (t // 2 + 1))
        RED(stats[:, t:t + 1], xt[t % 2][:, :], ADD, s_red)
        ACT(stg[t % 2][:, :], xt[t % 2][:, :], AF.Square,
            accum=stats[:, 16 + t:17 + t], sem=s_sqa)
        wt("vector", s_red, t + 1)
        wt("vector", s_sqa, t + 1)
        TS(stats[:, 32 + t:33 + t], stats[:, t:t + 1], -1.0 / D, None, MUL, None, sem=s_stt)
        wt("vector", s_stt, s_stt.n)
        TT("vector", stats[:, 64 + t:65 + t], stats[:, 32 + t:33 + t],
           stats[:, 32 + t:33 + t], MUL, s_stt)
        TS(stats[:, 48 + t:49 + t], stats[:, 16 + t:17 + t], 1.0 / D, EPS, MUL, ADD, sem=s_stt)
        wt("vector", s_stt, s_stt.n)
        TT("vector", stats[:, 48 + t:49 + t], stats[:, 48 + t:49 + t],
           stats[:, 64 + t:65 + t], AluOpType.subtract, s_stt)
        wt("vector", s_stt, s_stt.n)
        RCP(stats[:, 48 + t:49 + t], stats[:, 48 + t:49 + t], s_stt)
        wt("scalar", s_stt, s_stt.n)
        ACT(stats[:, 48 + t:49 + t], stats[:, 48 + t:49 + t], AF.Sqrt, sem=s_srt)
        wt("vector", s_srt, t + 1)
        if t >= 1:
            wt("vector", s_tr, 4 * t)
        TS(lnscr[:, :], xt[t % 2][:, :], stats[:, 32 + t:33 + t], stats[:, 48 + t:49 + t],
           ADD, MUL, sem=s_nrm)
        for j in range(4):
            half = (t * 4 + j) % 2
            wt("tensor", s_nrm, t + 1)
            if P1_TRB >= 2:
                wt("tensor", s_tre, P1_TRB - 1)
            for i in range(4):
                dc = 4 * j + i
                TR(psA[:, half * 512 + i * 128: half * 512 + (i + 1) * 128],
                   lnscr[:, dc * 128:(dc + 1) * 128],
                   sem=(s_tr if i == 3 else None))
            wt("vector", s_tr, s_tr.n)
            dst = xln1.rearrange("p (dt c) -> p dt c", c=T)[:, 4 * j:4 * j + 4, t * 128:(t + 1) * 128]
            CPY("vector", dst, psA.rearrange("p (q c) -> p q c", c=128)[:, half * 4:half * 4 + 4, :], s_tre)
            P1_TRB += 1

    def hbase(h):
        return (h % 2) * 3088 + (h // 2) * 386

    slices = []
    for h in list(range(0, H, 2)) + list(range(1, H, 2)):
        slices.append(("q", h))
        slices.append(("k", h))
        slices.append(("v", h))
    s_wldp = [S(f"wld{i}") for i in range(4)]
    s_st2p = [S("st0"), S("st1")]
    pend_norm = []
    vslice_cnt = 0
    nrmrow_cnt = 0

    def flush_norms():
        nonlocal nrmrow_cnt
        for fn in pend_norm:
            fn()
        pend_norm.clear()

    s_cc1e = S("cc1e")
    s_cc1o = S("cc1o")
    for si, (typ, h) in enumerate(slices):
        if si == 26:
            flush_norms()
            wt("gpsimd", s_st2p[0], s_st2p[0].n)
            wt("gpsimd", s_st2p[1], s_st2p[1].n)
            wt("gpsimd", s_nst, s_nst.n)
            run("gpsimd", lambda E: E.collective_compute(
                "AllGather", AluOpType.bypass, replica_groups=[list(range(NCORES))],
                ins=[kvq[0:3088, :].opt()], outs=[g1e.opt()]), s_cc1e, 1)
        col = {"q": 0, "k": D, "v": 2 * D}[typ] + h * 128
        if si >= 4:
            wt("gpsimd", s_qps, 4 * (si - 3))
        src = wqkvT[:, col:col + 128].rearrange("(dt p) e -> p dt e", p=128)
        dstw = wsb[si % 4][:, :].rearrange("p (dt e) -> p dt e", e=128)
        DMA("gpsimd", dstw, src, s_wldp[si % 4])
        wt("tensor", s_wldp[si % 4], 16 * (si // 4 + 1))
        if si == 0:
            wt("tensor", s_tre, P1_TRB)
            wt("tensor", s_cst, N_CST)
        for tc in range(4):
            if si > 0:
                wt("tensor", s_ev, 4 * (si - 1) + tc + 1)
            for dt in range(16):
                MM(psA[:, tc * 512:(tc + 1) * 512],
                   wsb[si % 4][:, dt * 128:(dt + 1) * 128],
                   xln1[:, dt * T + tc * 512: dt * T + (tc + 1) * 512],
                   start=(dt == 0), stop=(dt == 15),
                   sem=(s_qps if dt == 15 else None))
        flush_norms()
        if si >= 2:
            wt("vector", s_st2p[si % 2], 16 * ((si - 2) // 2 + 1))
        if typ == "v":
            wt("vector", s_vt, 4 * vslice_cnt)
        for tc in range(4):
            wt("vector", s_qps, 4 * si + tc + 1)
            qd = psA[:, tc * 512:(tc + 1) * 512]
            if typ == "q":
                TS(stg[si % 2][:, tc * 512:(tc + 1) * 512], qd,
                   bq[:, 0 + h:1 + h], ISC, ADD, MUL, sem=s_ev)
            elif typ == "k":
                TS(stg[si % 2][:, tc * 512:(tc + 1) * 512], qd,
                   bq[:, 16 + h:17 + h], None, ADD, sem=s_ev)
            else:
                TS(vstage[:, tc * 512:(tc + 1) * 512], qd,
                   bq[:, 32 + h:33 + h], None, ADD, sem=s_ev)
        if typ in ("q", "k"):
            rowbase = (hbase(h) + 256) if typ == "q" else hbase(h)
            wt("sync", s_ev, 4 * (si + 1))
            DMA("sync", kvq[rowbase:rowbase + 128, :], stg[si % 2][:, :], s_st2p[si % 2])
            par = (si // 3 * 2 + (0 if typ == "q" else 1)) % 2
            wt("vector", s_ev, 4 * (si + 1))
            if nrmrow_cnt >= 2:
                wt("vector", s_nq, 4 * (nrmrow_cnt - 1))
            TT("vector", q2scr[par][:, :], stg[si % 2][:, :], stg[si % 2][:, :], MUL, s_sqr)
            myrow = (hbase(h) + 385) if typ == "q" else (hbase(h) + 384)
            mycnt = nrmrow_cnt
            mysqr = s_sqr.n

            def mk_norm(par=par, myrow=myrow, mycnt=mycnt, mysqr=mysqr):
                wt("tensor", s_sqr, mysqr)
                for j in range(4):
                    bank = j % 2
                    if j >= 2:
                        wt("tensor", s_nev, 4 * mycnt + j - 1)
                    MM(psB[0:1, 512 * (1 + bank):512 * (1 + bank) + 512],
                       onec[:, :], q2scr[par][:, j * 512:(j + 1) * 512],
                       start=True, stop=True, sem=s_nq)
                for j in range(4):
                    bank = j % 2
                    wt("vector", s_nq, 4 * mycnt + j + 1)
                    CPY("vector", nrow[0:1, j * 512:(j + 1) * 512],
                        psB[0:1, 512 * (1 + bank):512 * (1 + bank) + 512], s_nev)
                wt("gpsimd", s_nev, 4 * (mycnt + 1))
                DMA("gpsimd", kvq[myrow:myrow + 1, :], nrow[0:1, :], s_nst)
            if nrmrow_cnt >= 1:
                def mk_guard(mc=nrmrow_cnt):
                    wt("vector", s_nst, 16 * mc)
                pend_norm.append(mk_guard)
            pend_norm.append(mk_norm)
            nrmrow_cnt += 1
        else:
            myev = s_ev.n
            wt("tensor", s_ev, myev)
            for j in range(4):
                if vslice_cnt * 4 + j >= 1:
                    wt("tensor", s_vte, vslice_cnt * 4 + j)
                for i in range(4):
                    kb = 4 * j + i
                    TR(psB[:, i * 128:(i + 1) * 128], vstage[:, kb * 128:(kb + 1) * 128],
                       sem=(s_vt if i == 3 else None))
                wt("vector", s_vt, s_vt.n)
                if j == 0 and si >= 2:
                    wt("vector", s_st2p[si % 2], 16 * ((si - 2) // 2 + 1))
                CPY("vector", stg[si % 2][:, j * 512:(j + 1) * 512], psB[:, 0:512], s_vte)
            vslice_cnt += 1
            wt("sync", s_vte, s_vte.n)
            DMA("sync", kvq[hbase(h) + 128: hbase(h) + 256, :], stg[si % 2][:, :], s_st2p[si % 2])
    flush_norms()
    wt("gpsimd", s_st2p[0], s_st2p[0].n)
    wt("gpsimd", s_st2p[1], s_st2p[1].n)
    wt("gpsimd", s_nst, s_nst.n)
    run("gpsimd", lambda E: E.collective_compute(
        "AllGather", AluOpType.bypass, replica_groups=[list(range(NCORES))],
        ins=[kvq[3088:6176, :].opt()], outs=[g1o.opt()]), s_cc1o, 1)

    if phase_stop <= 2:
        wt("sync", s_cc1e, 1)
        wt("sync", s_cc1o, 1)
        wt("sync", s_st2p[0], s_st2p[0].n)
        wt("sync", s_st2p[1], s_st2p[1].n)
        wt("sync", s_nst, s_nst.n)
        return finalize(nc, ops)

    s_cc2e = S("cc2e")
    s_cc2oa = S("cc2oa")
    s_cc2ob = S("cc2ob")
    s_pcc = [S("pcc0"), S("pcc1")]
    s_pst = [S("pst0"), S("pst1")]
    pc_tiles = []
    for wi, (wsrc_, wdst_, R, C) in enumerate((
            (woutT, woutb, D, D), (wgT, wgb, D, DH), (wuT, wub, D, DH),
            (wdT, wdb, DH, D))):
        for r0 in range(0, R, 256):
            for c0 in range(0, C, 2048):
                pc_tiles.append((wsrc_, wdst_, r0, c0, wi == 0))
    pc_state = {"i": 0, "store_done": []}

    def emit_precast(ntiles, cap=10**9):
        for _ in range(ntiles):
            i = pc_state["i"]
            if i >= len(pc_tiles):
                return
            pc_state["i"] = i + 1
            ws, wdst_, r0, c0, is_wout = pc_tiles[i]
            par_ = i % 2
            if i >= 8:
                thr = min(cap, int((i - 8) * TOTAL_PAIRS / 100))
                if thr > 0:
                    wt("gpsimd", s_av, thr)
            if i >= 2:
                gsem, gtgt = pc_state["store_done"][i - 2]
                wt("gpsimd", gsem, gtgt)
            DMA("gpsimd",
                pcst[par_][:, :].rearrange("p (t c) -> p t c", c=2048),
                ws[r0:r0 + 256, c0:c0 + 2048].rearrange("(t p) c -> p t c", p=128),
                s_pcc[par_])
            wt("gpsimd", s_pcc[par_], s_pcc[par_].n)
            DMA("gpsimd",
                wdst_[r0:r0 + 256, c0:c0 + 2048].rearrange("(t p) c -> p t c", p=128),
                pcst[par_][:, :].rearrange("p (t c) -> p t c", c=2048), s_pst[par_])
            pc_state["store_done"].append((s_pst[par_], s_pst[par_].n))

    pair_g = 0
    gq = 0
    pairs_done_slot = []

    wt("gpsimd", s_qps, s_qps.n)

    s_kvgp = [S("kvg0"), S("kvg1")]
    kvg_tgts = {}

    def emit_gather(slot):
        b = slot % 2
        ic = 13 * slot
        s_kvg = s_kvgp[b]
        g1v = g1e if slot < 2 else g1o
        wt("gpsimd", s_cc1e if slot < 2 else s_cc1o, 1)
        if slot >= 2:
            wt("gpsimd", s_av, pairs_done_slot[slot - 2])
        for i in range(4):
            IDMA(k_sb[b][:, i * 2048:(i + 1) * 2048], g1v, idxs[:, ic + i:ic + i + 1], 0, s_kvg)
        for i in range(4):
            IDMA(v_sb[b][:, i * 2048:(i + 1) * 2048], g1v, idxs[:, ic + 4 + i:ic + 5 + i], 0, s_kvg)
        for i in range(4):
            IDMA(q_sb[b][:, i * 2048:(i + 1) * 2048], g1v, idxs[:, ic + 8 + i:ic + 9 + i], 0, s_kvg)
        IDMA(kq2[:, b * 2048:(b + 1) * 2048], g1v, idxs[0:8, ic + 12:ic + 13], 0, s_kvg)
        kvg_tgts[slot] = s_kvg.n

    for slot in range(4):
        ss, par = [(0, 0), (1, 0), (0, 1), (1, 1)][slot]
        b = slot % 2
        ic = 13 * slot
        s_kvg = s_kvgp[b]
        g1v = g1e if par == 0 else g1o
        if slot == 0:
            emit_gather(0)
            emit_precast(8, cap=0)
            emit_gather(1)
            emit_precast(32, cap=PAIRS_PER_SLOT)
        elif slot == 2:
            emit_gather(2)
            wt("gpsimd", s_ast, 2 * 256)
            run("gpsimd", lambda E: E.collective_compute(
                "AllGather", AluOpType.bypass, replica_groups=[list(range(NCORES))],
                ins=[in2[0:256, :].opt()], outs=[g2e.opt()]), s_cc2e, 1)
            emit_precast(32, cap=2 * PAIRS_PER_SLOT)
            emit_gather(3)
            emit_precast(32, cap=3 * PAIRS_PER_SLOT)
            wt("gpsimd", s_ast, 3 * 256)
            run("gpsimd", lambda E: E.collective_compute(
                "AllGather", AluOpType.bypass, replica_groups=[list(range(NCORES))],
                ins=[in2[256:384, :].opt()], outs=[g2o[0:NCORES * 128, :].opt()]),
                s_cc2oa, 1)
        kvg_tgt = kvg_tgts[slot]
        wt("vector", s_kvg, kvg_tgt)
        RED(sml[0:8, 0:1], kq2[:, b * 2048:(b + 1) * 2048], AluOpType.max, s_mch)
        wt("sync", s_mch, s_mch.n)
        DMA("sync", nrm_d[0:1, 0:8], sml[0:8, 0:1], s_mch)
        wt("sync", s_mch, s_mch.n)
        DMA("sync", sml[0:1, 8:16], nrm_d[0:1, 0:8], s_mch)
        wt("vector", s_mch, s_mch.n)
        RED(sml[0:1, 0:1], sml[0:1, 8:12], AluOpType.max, s_mch)
        RED(sml[0:1, 1:2], sml[0:1, 12:16], AluOpType.max, s_mch)
        wt("vector", s_mch, s_mch.n)
        TS(sml[0:1, 2:3], sml[0:1, 0:1], sml[0:1, 1:2], 1e-20, MUL, ADD, sem=s_mch)
        wt("scalar", s_mch, s_mch.n)
        ACT(sml[0:1, 3:4], sml[0:1, 2:3], AF.Ln, sem=s_mch)
        wt("vector", s_mch, s_mch.n)
        TS(sml[0:1, 4:5], sml[0:1, 3:4], 0.5, None, MUL, None, sem=s_mch)
        wt("scalar", s_mch, s_mch.n)
        ACT(sml[0:1, 5:6], sml[0:1, 4:5], AF.Exp, sem=s_mch)
        wt("vector", s_mch, s_mch.n)
        TS(sml[0:1, 6:7], sml[0:1, 5:6], -1.0, None, MUL, None, sem=s_mch)
        wt("tensor", s_mch, s_mch.n)
        wt("tensor", s_ex, pair_g)
        MM(psA[:, 0:1], oner[:, :], sml[0:1, 6:7], start=True, stop=True, sem=s_bc)
        wt("vector", s_bc, s_bc.n)
        CPY("vector", negm[:, :], psA[:, 0:1], s_mch)
        negm_tgt = s_mch.n
        wt("tensor", s_kvg, kvg_tgt)
        wt("tensor", s_mch, negm_tgt)
        wt("scalar", s_mch, negm_tgt)
        wt("vector", s_kvg, kvg_tgt)
        if slot >= 2:
            wt("vector", s_ast, s_ast.n)
        deferred_div = []

        def flushdiv():
            for fn in deferred_div:
                fn()
            deferred_div.clear()

        plist = []
        for qs in range(16):
            npair = 2 * (qs + 1)
            for p in range(npair):
                plist.append((qs, p, npair))
        pair_g0 = pair_g
        gq0 = gq
        LEAD = 2
        post_pe = deferred_div

        def emit_S(idx, b=b):
            qs, p, npair = plist[idx]
            g = pair_g0 + idx
            stoff = (g % 2) * 1024
            if g >= 2:
                wt("tensor", s_ex, g - 1)
            for half in range(2):
                kb = 2 * p + half
                v = kb - 4 * qs
                masked = v >= 0
                MM(psA[:, stoff + half * 512: stoff + (half + 1) * 512],
                   k_sb[b][:, kb * 128:(kb + 1) * 128],
                   q_sb[b][:, qs * 512:(qs + 1) * 512],
                   start=True, stop=(not masked),
                   sem=(s_sq if (half == 1 and not masked) else None))
                if masked:
                    MM(psA[:, stoff + half * 512: stoff + (half + 1) * 512],
                       idnb[:, :], masks[:, v * 512:(v + 1) * 512],
                       start=False, stop=True,
                       sem=(s_sq if half == 1 else None))
            for f in post_pe:
                f()
            post_pe.clear()

        def emit_exp(idx):
            qs, p, npair = plist[idx]
            g = pair_g0 + idx
            stoff = (g % 2) * 1024
            wt("scalar", s_sq, g + 1)
            if g >= 3:
                wt("scalar", s_av, g - 3)
            ACT(pt[g % 4][:, :], psA[:, stoff:stoff + 1024], AF.Exp,
                bias=negm[:, 0:1], sem=s_ex)

        def emit_AV(idx, b=b, ss=ss, par=par):
            qs, p, npair = plist[idx]
            g = pair_g0 + idx
            gq_abs = gq0 + qs
            ob = qs % 2
            lb_l = 1024 + (qs % 2) * 512
            lb_b = 1024 + ((qs + 1) % 2) * 512
            wt("tensor", s_ex, g + 1)
            if p == 0:
                if gq_abs >= 2:
                    wt("tensor", s_div, gq_abs - 1)
                wt("tensor", s_lev, gq_abs)
            for half in range(2):
                kb = 2 * p + half
                MM(psB[:, ob * 512:(ob + 1) * 512],
                   v_sb[b][:, kb * 128:(kb + 1) * 128],
                   pt[g % 4][:, half * 512:(half + 1) * 512],
                   start=(p == 0 and half == 0), stop=(p == npair - 1 and half == 1))
            for half in range(2):
                MM(psB[0:1, lb_l:lb_l + 512], onec[:, :],
                   pt[g % 4][:, half * 512:(half + 1) * 512],
                   start=(p == 0 and half == 0), stop=(p == npair - 1 and half == 1),
                   sem=(s_av if half == 1 else None))
            if p == npair - 1:
                wt("vector", s_av, g + 1)
                RCP(linv[:, :], psB[0:1, lb_l:lb_l + 512], s_rcp)
                myrcp = s_rcp.n

                def mk_bcast(myrcp=myrcp, lb_b=lb_b):
                    wt("tensor", s_rcp, myrcp)
                    MM(psB[:, lb_b:lb_b + 512], oner[:, :], linv[:, :],
                       start=True, stop=True, sem=s_bc)
                post_pe.append(mk_bcast)
                wt("scalar", s_bc, s_bc.n + 1)
                ACT(linb[:, :], psB[:, lb_b:lb_b + 512], AF.Copy, sem=s_lev)
                wt("vector", s_lev, s_lev.n)
                TT("vector", ast[b][:, qs * 512:(qs + 1) * 512],
                   psB[:, ob * 512:(ob + 1) * 512], linb[:, :], MUL, s_div)
                wt("sync", s_div, s_div.n)
                DMA("sync", in2[(2 * par + ss) * 128:(2 * par + ss) * 128 + 128,
                                ss * SQ + qs * 512: ss * SQ + (qs + 1) * 512],
                    ast[b][:, qs * 512:(qs + 1) * 512], s_ast)

        NP = len(plist)
        for idx in range(NP):
            emit_S(idx)
            emit_exp(idx)
            if idx >= LEAD:
                emit_AV(idx - LEAD)
        for idx in range(NP - LEAD, NP):
            emit_AV(idx)
        pair_g += NP
        gq += 16
        for fn in deferred_div:
            fn()
        deferred_div.clear()
        pairs_done_slot.append(s_av.n)

    wt("gpsimd", s_ast, 4 * 256)
    run("gpsimd", lambda E: E.collective_compute(
        "AllGather", AluOpType.bypass, replica_groups=[list(range(NCORES))],
        ins=[in2[384:512, :].opt()], outs=[g2o[NCORES * 128:, :].opt()]), s_cc2ob, 1)

    if phase_stop <= 4:
        wt("sync", s_cc2e, 1)
        wt("sync", s_cc2oa, 1)
        wt("sync", s_cc2ob, 1)
        wt("sync", s_ast, s_ast.n)
        return finalize(nc, ops)

    g2ev = g2e.rearrange("r (blk t) -> r blk t", t=2048)
    g2ov = g2o.rearrange("r (blk t) -> r blk t", t=2048)
    wt("gpsimd", s_cc2e, 1)
    wt("gpsimd", s_div, s_div.n)
    for dt in range(0, 16, 2):
        IDMA(afm[:, dt * T:(dt + 1) * T], g2ev, idxs[:, 52 + dt:53 + dt], 1, s_pg)
    wt("gpsimd", s_cc2oa, 1)
    wt("gpsimd", s_cc2ob, 1)
    for dt in range(1, 16, 2):
        IDMA(afm[:, dt * T:(dt + 1) * T], g2ov, idxs[:, 52 + dt:53 + dt], 1, s_pg)
    emit_precast(len(pc_tiles))
    s_xrp = [S("xr0"), S("xr1")]
    for tc0 in range(2):
        wt("sync", s_av, s_av.n)
        DMA("sync", xres[tc0][:, :], x_in[tc0 * 128:(tc0 + 1) * 128, :], s_xrp[tc0])
    for ssem_, stgt_ in pc_state["store_done"][6:8]:
        wt("sync", ssem_, stgt_)
    DMA("sync", wout[:, :].rearrange("p (dt e) -> p dt e", e=D),
        woutb.rearrange("(dt p) e -> p dt e", p=128), s_pg)
    pg_tgt = s_pg.n
    wt("tensor", s_ex, s_ex.n)
    wt("tensor", s_lev, s_lev.n)
    wt("tensor", s_div, s_div.n)
    for j in range(4):
        MM(psB[:, j * 512:(j + 1) * 512], oner[:, :], bor[0:1, j * 512:(j + 1) * 512],
           start=True, stop=True, sem=s_bcb)
    wt("scalar", s_bcb, s_bcb.n)
    wt("scalar", s_pg, 16)
    ACT(bob[:, :], psB[:, :], AF.Copy, sem=s_bcb)
    bcb_tgt = s_bcb.n
    wt("tensor", s_pg, pg_tgt)
    wt("tensor", s_bcb, bcb_tgt)
    wt("vector", s_bcb, bcb_tgt)
    for tc in range(16):
        ps = psA if tc % 2 == 0 else psB
        if tc >= 2:
            wt("sync", s_a2, tc - 1)
            DMA("sync", xres[tc % 2][:, :], x_in[tc * 128:(tc + 1) * 128, :], s_xrp[tc % 2])
        if tc >= 2:
            wt("tensor", s_a1, tc - 1)
        for e4 in range(4):
            for dt in range(16):
                MM(ps[:, e4 * 512:(e4 + 1) * 512],
                   afm[:, dt * T + tc * 128: dt * T + tc * 128 + 128],
                   wout[:, dt * D + e4 * 512: dt * D + (e4 + 1) * 512],
                   start=(dt == 0), stop=(dt == 15),
                   sem=(s_y if e4 == 3 and dt == 15 else None))
        wt("vector", s_y, tc + 1)
        if tc >= 1:
            wt("vector", s_x2s, 16 * tc)
        TT("vector", x2st[:, :], ps[:, :], bob[:, :], ADD, s_a1)
        wt("vector", s_xrp[tc % 2], 16 * (tc // 2 + 1))
        wt("vector", s_a1, s_a1.n)
        TT("vector", x2st[:, :], x2st[:, :], xres[tc % 2][:, :], ADD, s_a2)
        wt("sync", s_a2, s_a2.n)
        DMA("sync", x2b[tc * 128:(tc + 1) * 128, :], x2st[:, :], s_x2s)

    if phase_stop <= 5:
        wt("sync", s_x2s, s_x2s.n)
        finish(ops)
        return finalize(nc, ops)

    s_dhc = S("dhc")
    s_gus = [S(f"gus{r}") for r in range(4)]
    s_dws = [S(f"dws{r}") for r in range(2)]
    gu_rel = [0, 0, 0, 0]
    dchunks = [(hf_, sup_, e4_, hc_) for hf_ in range(2) for sup_ in range(4)
               for e4_ in range(4) for hc_ in range(16)]
    NDCH = len(dchunks)
    dptr = [0]
    demit = [0]
    dload = {}
    dcomp = {}
    for hf in range(2):
        tok0 = hf * 1024
        if hf == 0:
            wt("sync", s_a2, s_a2.n)
            wt("sync", s_y, s_y.n)
        for tt in range(8):
            wt("sync", s_x2s, 256)
            if hf == 1 and tt == 0:
                wt("sync", s_ye, 128)
                wt("sync", s_out, 16 * 8)
            DMA("sync", y2[:, tt * D:(tt + 1) * D],
                x2b[(tok0 // 128 + tt) * 128:(tok0 // 128 + tt + 1) * 128, :], s_x2l)
        if "p6s" in dbg and hf == 1:
            wt("vector", s_cst, dbg_dump_tgt)
            wt("scalar", s_cst, dbg_dump_tgt)
        red0, sqa0 = s_red.n, s_sqa.n
        wt("vector", s_x2l, 16 * 8 * (hf + 1))
        wt("scalar", s_x2l, 16 * 8 * (hf + 1))
        if hf == 0:
            wt("scalar", s_pst[0], s_pst[0].n)
            wt("scalar", s_pst[1], s_pst[1].n)
        for tt in range(8):
            if tt >= 1:
                wt("scalar", s_sqa, sqa0 + tt)
            RED(stats[:, tt:tt + 1], y2[:, tt * D:(tt + 1) * D], ADD, s_red)
            ACT(lnscr2[:, :], y2[:, tt * D:(tt + 1) * D], AF.Square,
                accum=stats[:, 16 + tt:17 + tt], sem=s_sqa)
        wt("vector", s_red, red0 + 8)
        wt("vector", s_sqa, sqa0 + 8)
        TS(stats[:, 32:40], stats[:, 0:8], -1.0 / D, None, MUL, None, sem=s_stt)
        wt("vector", s_stt, s_stt.n)
        TT("vector", stats[:, 64:72], stats[:, 32:40], stats[:, 32:40], MUL, s_stt)
        TS(stats[:, 48:56], stats[:, 16:24], 1.0 / D, EPS, MUL, ADD, sem=s_stt)
        wt("vector", s_stt, s_stt.n)
        TT("vector", stats[:, 48:56], stats[:, 48:56], stats[:, 64:72], AluOpType.subtract, s_stt)
        wt("vector", s_stt, s_stt.n)
        RCP(stats[:, 48:56], stats[:, 48:56], s_stt)
        wt("scalar", s_stt, s_stt.n)
        ACT(stats[:, 48:56], stats[:, 48:56], AF.Sqrt, sem=s_srt)
        srt_tgt = s_srt.n
        trb0 = s_tr.n
        nrm0 = s_nrm.n
        tre0 = s_tre.n
        for tt in range(8):
            wt("vector", s_srt, srt_tgt)
            wt("vector", s_sqa, sqa0 + 8)
            if tt >= 1:
                wt("vector", s_tr, trb0 + 4 * tt)
            TS(lnscr2[:, :], y2[:, tt * D:(tt + 1) * D],
               stats[:, 32 + tt:33 + tt], stats[:, 48 + tt:49 + tt], ADD, MUL, sem=s_nrm)
            for j in range(4):
                half = (tt * 4 + j) % 2
                wt("tensor", s_nrm, nrm0 + tt + 1)
                if tt == 0 and j == 0:
                    wt("tensor", s_a1, s_a1.n)
                    wt("tensor", s_x2l, 16 * (8 * hf + 8))
                    wt("tensor", s_ye, s_ye.n)
                if (tt * 4 + j) >= 1:
                    wt("tensor", s_tre, tre0 + (tt * 4 + j))
                for i in range(4):
                    dc = 4 * j + i
                    TR(psB[:, half * 512 + i * 128: half * 512 + (i + 1) * 128],
                       lnscr2[:, dc * 128:(dc + 1) * 128],
                       sem=(s_tr if i == 3 else None))
                wt("vector", s_tr, s_tr.n)
                dst = xln2.rearrange("p (dt c) -> p dt c", c=1024)[:, 4 * j:4 * j + 4, tt * 128:(tt + 1) * 128]
                CPY("vector", dst, psB.rearrange("p (q c) -> p q c", c=128)[:, half * 4:half * 4 + 4, :], s_tre)
        xln2_tre = s_tre.n
        if "p6s" in dbg and hf == 0:
            dbg_st2 = nc.dram_tensor("dbg_st2", [128, 80], F32, kind="ExternalOutput").ap()
            dbg_y2 = nc.dram_tensor("dbg_y2", [128, 8 * D], F32, kind="ExternalOutput").ap()
            wt("sync", s_srt, s_srt.n)
            DMA("sync", dbg_st2[:, :], stats[:, :], s_cst)
            wt("sync", s_x2l, 16 * 8)
            DMA("sync", dbg_y2[:, :], y2[:, :], s_cst)
            dbg_dump_tgt = s_cst.n
        if "p6i" in dbg and hf == 0:
            dbg_xln2 = nc.dram_tensor("dbg_xln2", [128, 16 * 1024], BF16, kind="ExternalOutput").ap()
            wt("sync", s_tre, xln2_tre)
            DMA("sync", dbg_xln2[:, :], xln2[:, :], s_cst)
        for sup in range(4):
            gmm0 = s_gmm.n
            next_load = 0
            gld_done = {}

            def emit_gu(j2):
                nonlocal next_load
                ring = j2 % 2
                wt("scalar", s_gmm, gu_rel[ring])
                if pc_state["i"] and gu_rel[ring] == 0:
                    wt("scalar", s_pst[0], s_pst[0].n)
                    wt("scalar", s_pst[1], s_pst[1].n)
                h0 = sup * 2048 + j2 * 256
                tgts = []
                for gi, wsrc in ((0, wgb), (1, wub)):
                    DMA("scalar",
                        gu[ring][:, gi * 4096:(gi + 1) * 4096].rearrange("p (dt e) -> p dt e", e=256),
                        wsrc[:, h0:h0 + 256].rearrange("(dt p) e -> p dt e", p=128),
                        s_gus[ring])
                    tgts.append((s_gus[ring], s_gus[ring].n))
                gu_rel[ring] = gmm0 + 4 * (2 * j2 + 1) + 4
                gld_done[j2] = tgts
                next_load = j2 + 1

            for hc in range(16):
                while next_load <= min(hc // 2 + 1, 7):
                    emit_gu(next_load)
                j2 = hc // 2
                for gsem_, gtgt_ in gld_done[j2]:
                    wt("tensor", gsem_, gtgt_)
                if sup == 0 and hc == 0:
                    wt("tensor", s_tre, xln2_tre)
                wt("tensor", s_sil, s_sil.n)
                wt("tensor", s_gtd, s_gtd.n)
                if hc == 0:
                    wt("tensor", s_ye, s_ye.n)
                goff = (hc % 2) * 128
                for ts in range(2):
                    for dt in range(16):
                        MM(psA[:, ts * 512:(ts + 1) * 512],
                           gu[j2 % 2][:, dt * 256 + goff: dt * 256 + goff + 128],
                           xln2[:, dt * 1024 + ts * 512: dt * 1024 + (ts + 1) * 512],
                           start=(dt == 0), stop=(dt == 15),
                           sem=(s_gmm if dt == 15 else None))
                for ts in range(2):
                    for dt in range(16):
                        MM(psA[:, 1024 + ts * 512: 1024 + (ts + 1) * 512],
                           gu[j2 % 2][:, 4096 + dt * 256 + goff: 4096 + dt * 256 + goff + 128],
                           xln2[:, dt * 1024 + ts * 512: dt * 1024 + (ts + 1) * 512],
                           start=(dt == 0), stop=(dt == 15),
                           sem=(s_gmm if dt == 15 else None))
                wt("scalar", s_gmm, gmm0 + hc * 4 + 2)
                wt("scalar", s_gtd, s_gtd.n)
                ACT(lnscr2[:, 0:1024], psA[:, 0:1024], AF.Silu, sem=s_sil)
                wt("vector", s_gmm, gmm0 + hc * 4 + 4)
                wt("vector", s_sil, s_sil.n)
                TT("vector", gated[:, hc * 1024:(hc + 1) * 1024],
                   lnscr2[:, 0:1024], psA[:, 1024:2048], MUL, s_gtd)
            gtd_tgt = s_gtd.n
            if "p6i" in dbg and hf == 0 and sup == 0:
                dbg_gtd = nc.dram_tensor("dbg_gtd", [128, 16 * 1024], BF16, kind="ExternalOutput").ap()
                wt("sync", s_gtd, gtd_tgt)
                DMA("sync", dbg_gtd[:, :], gated[:, :], s_cst)
            for e4 in range(4):
                for hc in range(16):
                    n = dptr[0] + e4 * 16 + hc
                    while demit[0] <= min(n + 1, NDCH - 1):
                        m = demit[0]
                        mhf, msup, me4, mhc = dchunks[m]
                        if m >= 2:
                            gsem, gtgt = dcomp[m - 2]
                            wt("scalar", gsem, gtgt)
                        else:
                            wt("scalar", s_pst[0], s_pst[0].n)
                            wt("scalar", s_pst[1], s_pst[1].n)
                        DMA("scalar", dw[m % 2][:, :],
                            wdb[(msup * 16 + mhc) * 128:(msup * 16 + mhc + 1) * 128,
                                me4 * 512:(me4 + 1) * 512], s_dws[m % 2])
                        dload[m] = (s_dws[m % 2], s_dws[m % 2].n)
                        demit[0] += 1
                    wt("tensor", dload[n][0], dload[n][1])
                    if e4 == 0:
                        wt("tensor", s_gtd, gtd_tgt - 16 + hc + 1)
                        if hc == 0:
                            wt("tensor", s_sil, s_sil.n)
                            wt("tensor", s_ye, s_ye.n)
                    elif hc == 0:
                        wt("tensor", s_ye, s_ye.n)
                    for tc in range(8):
                        ps = psB if tc < 4 else psA
                        qd = tc % 4
                        sem = None
                        if hc == 15:
                            sem = s_dmm
                        elif tc == 7:
                            sem = s_dhc
                        MM(ps[:, qd * 512:(qd + 1) * 512],
                           gated[:, hc * 1024 + tc * 128: hc * 1024 + tc * 128 + 128],
                           dw[n % 2][:, :],
                           start=(hc == 0), stop=(hc == 15), sem=sem)
                    dcomp[n] = (s_dhc, s_dhc.n) if hc < 15 else (s_dmm, s_dmm.n)
                for tc in range(8):
                    ps = psB if tc < 4 else psA
                    qd = tc % 4
                    wt("vector", s_dmm, s_dmm.n - 8 + tc + 1)
                    TT("vector", y2[:, tc * D + e4 * 512: tc * D + (e4 + 1) * 512],
                       y2[:, tc * D + e4 * 512: tc * D + (e4 + 1) * 512],
                       ps[:, qd * 512:(qd + 1) * 512], ADD, s_ye)
            dptr[0] += 64
        ye_end = s_ye.n
        for tt in range(8):
            wt("sync", s_ye, ye_end - 8 + tt + 1)
            DMA("sync", out_d[(tok0 // 128 + tt) * 128:(tok0 // 128 + tt + 1) * 128, :],
                y2[:, tt * D:(tt + 1) * D], s_out)

    wt("sync", s_out, s_out.n)
    return finalize(nc, ops)


def finish(ops):
    pass


def finalize(nc, ops):
    blk = nc.Block().__enter__()
    for name in ("tensor", "vector", "scalar", "gpsimd", "sync"):
        lst = ops[name]

        def runner(E, lst=lst):
            for f in lst:
                f(E)
        getattr(blk, name)(runner)
    blk.__exit__(None, None, None)
    return nc


def _prep_inputs(L0_attn_x, L0_attn_w_qkv, L0_attn_b_qkv, L0_attn_w_out,
                 L0_attn_b_out, L0_ffn_w_gate, L0_ffn_w_up, L0_ffn_w_down):
    x = np.asarray(L0_attn_x, dtype=np.float32)
    wqkvT = np.ascontiguousarray(np.asarray(L0_attn_w_qkv, dtype=np.float32).T)
    woutT = np.ascontiguousarray(np.asarray(L0_attn_w_out, dtype=np.float32).T)
    wgT = np.ascontiguousarray(np.asarray(L0_ffn_w_gate, dtype=np.float32).T)
    wuT = np.ascontiguousarray(np.asarray(L0_ffn_w_up, dtype=np.float32).T)
    wdT = np.ascontiguousarray(np.asarray(L0_ffn_w_down, dtype=np.float32).T)
    bq = np.ascontiguousarray(
        np.asarray(L0_attn_b_qkv, dtype=np.float32).reshape(48, 128).T)
    bor = np.ascontiguousarray(np.asarray(L0_attn_b_out, dtype=np.float32)[None, :])
    masks = np.zeros((128, 2048), dtype=np.float32)
    ki = np.arange(128)[:, None]
    qj = np.arange(512)[None, :]
    for v in range(4):
        masks[:, v * 512:(v + 1) * 512] = np.where(v * 128 + ki <= qj, 0.0, NEGBIG)
    masks = masks.astype(ml_dtypes.bfloat16)
    idn = np.eye(128, dtype=np.float32)
    idnb = np.eye(128, dtype=ml_dtypes.bfloat16)
    onec = np.ones((128, 1), dtype=ml_dtypes.bfloat16)
    oner = np.ones((1, 128), dtype=np.float32)

    shared = dict(wqkvT=wqkvT, woutT=woutT, wgT=wgT, wuT=wuT, wdT=wdT,
                  bqd=bq, bord=bor, masksd=masks, idnd=idn, idnbd=idnb,
                  onecd=onec, onerd=oner)
    in_maps = []
    for c in range(NCORES):
        s = c // 4
        q0 = (c % 4) * T
        idx = np.zeros((128, 72), dtype=np.uint32)
        ar = np.arange(128, dtype=np.uint32)
        for slot in range(4):
            ss, par = [(0, 0), (1, 0), (0, 1), (1, 1)][slot]
            h = 2 * c + par
            ic = 13 * slot
            hb = (h // 2) * 386
            for i in range(4):
                prow = (ss * 4 + i) * 3088
                idx[:, ic + i] = prow + hb + ar
                idx[:, ic + 4 + i] = prow + hb + 128 + ar
                idx[:, ic + 8 + i] = prow + hb + 256 + ar
                idx[i, ic + 12] = prow + hb + 384
                idx[4 + i, ic + 12] = prow + hb + 385
        for dt in range(16):
            hh = dt
            if hh % 2 == 0:
                rowbase = (hh // 2) * 256 + s * 128
            else:
                rowbase = s * 1024 + (hh // 2) * 128
            colblk = s * 4 + (c % 4)
            idx[:, 52 + dt] = (rowbase + ar) * 8 + colblk
        m = dict(shared)
        m["x"] = np.ascontiguousarray(x[s, q0:q0 + T, :])
        m["idxd"] = idx
        in_maps.append(m)
    return in_maps


def kernel(**inputs):
    in_maps = _prep_inputs(**inputs)
    nc = build()
    res = run_bass_kernel_spmd(nc, in_maps, core_ids=list(range(NCORES)))
    out = np.zeros((2, SQ, D), dtype=np.float32)
    for c in range(NCORES):
        s = c // 4
        q0 = (c % 4) * T
        out[s, q0:q0 + T, :] = np.asarray(res.results[c]["out"])
    return out
